# version 13
# baseline (speedup 1.0000x reference)
import contextlib
import numpy as np
import concourse.bass as bass
import concourse.mybir as mybir
from concourse.bass_utils import run_bass_kernel_spmd
from bass_rust import RuntimeValue

F32 = mybir.dt.float32
I32 = mybir.dt.int32
BF16 = mybir.dt.bfloat16
AF = mybir.ActivationFunctionType
ALU = mybir.AluOpType
AX = mybir.AxisListType

S = 2048
D = 1024
NH = 16
E = 8
NT = S // 128
ND = D // 128
NQC = S // 512
TMAX = 15
NCH = 4 * E + 1
HUGE = float(1 << 16)


class Plan:
    def __init__(self):
        self.steps = {"t": [], "s": [], "v": [], "g": [], "y": []}
        self.sem_val = {}

    def add(self, eng, fn):
        self.steps[eng].append(fn)

    def inc(self, sem_name, amount=1):
        v = self.sem_val.get(sem_name, 0) + amount
        self.sem_val[sem_name] = v
        return v


def build_nc():
    nc = bass.Bass("TRN2", target_bir_lowering=False)

    x_h = nc.dram_tensor("x", [S, D], F32, kind="ExternalInput").ap()
    wq_h = nc.dram_tensor("wq", [D, D], F32, kind="ExternalInput").ap()
    wk_h = nc.dram_tensor("wk", [D, D], F32, kind="ExternalInput").ap()
    wv_h = nc.dram_tensor("wv", [D, D], F32, kind="ExternalInput").ap()
    wo_h = nc.dram_tensor("wo", [D, D], F32, kind="ExternalInput").ap()
    wr_h = nc.dram_tensor("wr", [D, E], F32, kind="ExternalInput").ap()
    br_h = nc.dram_tensor("br", [1, E], F32, kind="ExternalInput").ap()
    w1_h = nc.dram_tensor("w1", [E, D, 4096], F32, kind="ExternalInput").ap()
    b1_h = nc.dram_tensor("b1", [E, 4096], F32, kind="ExternalInput").ap()
    w2_h = nc.dram_tensor("w2", [E, 4096, D], F32, kind="ExternalInput").ap()
    b2_h = nc.dram_tensor("b2", [E, D], F32, kind="ExternalInput").ap()
    id_h = nc.dram_tensor("ident", [128, 128], F32, kind="ExternalInput").ap()
    tril_h = nc.dram_tensor("tril", [128, 128], F32, kind="ExternalInput").ap()
    out_h = nc.dram_tensor("out", [S, D], F32, kind="ExternalOutput").ap()
    xnd_h = nc.dram_tensor("xnd", [S, D], BF16, kind="Internal").ap()
    w1bf_h = nc.dram_tensor("w1bf", [E * 4 * 128, 8 * 1024], BF16, kind="Internal").ap()
    w2bf_h = nc.dram_tensor("w2bf", [E * 128, 32 * 1024], BF16, kind="Internal").ap()
    lst_h = nc.dram_tensor("lst", [NCH * 512 + 1, 8], F32, kind="Internal").ap()

    P = Plan()
    SEMS = ["dX0", "dX1", "dA", "dC", "dWq", "dWo", "dW1a", "dW1b",
            "dW2a", "dW2b", "dLs0", "dLs1", "dG0", "dG1", "dSc",
            "dPa", "dPb", "dPoA", "dPoB", "dXd0", "dXd1", "dY0", "dY1", "dOut",
            "tT", "tS", "tV", "ph"]
    OWN = {"t": "tT", "s": "tS", "v": "tV"}

    with contextlib.ExitStack() as ctx:
        sem = {n: ctx.enter_context(nc.semaphore(n)) for n in SEMS}

        def W(eng_key, sname, val):
            if val > 0:
                P.add(eng_key, lambda e, s=sem[sname], v=val: e.wait_ge(s, v))

        def DR(eng_key):
            P.add(eng_key, lambda e: e.drain())

        def EMIT(eng_key, fn, inc_sem=None, inc_by=1):
            if inc_sem is None:
                P.add(eng_key, fn)
                return None
            v = P.inc(inc_sem, inc_by)
            P.add(eng_key, lambda e, f=fn, s=sem[inc_sem], a=inc_by: f(e).then_inc(s, a))
            return v

        bank_last = {}

        def bank_wait(eng_key, bank):
            la = bank_last.get(bank)
            if la is not None and la[0] != eng_key:
                W(eng_key, la[1], la[2])

        def bank_done(eng_key, bank, sname, val):
            bank_last[bank] = (eng_key, sname, val)

        gates = ctx.enter_context(nc.sbuf_tensor("gates", [128, NT * E], F32))
        id_f = ctx.enter_context(nc.sbuf_tensor("id_f", [128, 128], F32))
        id_b = ctx.enter_context(nc.sbuf_tensor("id_b", [128, 128], BF16))
        tril_b = ctx.enter_context(nc.sbuf_tensor("tril_b", [128, 128], BF16))
        br_t = ctx.enter_context(nc.sbuf_tensor("br_t", [128, E], F32))
        ones_b = ctx.enter_context(nc.sbuf_tensor("ones_b", [1, 128], BF16))
        eps_t = ctx.enter_context(nc.sbuf_tensor("eps_t", [128, 1], F32))
        lg_all = ctx.enter_context(nc.sbuf_tensor("lg_all", [128, NT * E], F32))
        ind_all = ctx.enter_context(nc.sbuf_tensor("ind_all", [128, NT * E], F32))
        ind2_all = ctx.enter_context(nc.sbuf_tensor("ind2_all", [128, NT * E], F32))
        iota_p = ctx.enter_context(nc.sbuf_tensor("iota_p", [128, 1], F32))
        iota_eb = ctx.enter_context(nc.sbuf_tensor("iota_eb", [128, E], F32))
        iok_b = ctx.enter_context(nc.sbuf_tensor("iok_b", [E, 16], F32))
        ones8 = ctx.enter_context(nc.sbuf_tensor("ones8", [8, 1], BF16))
        exsched = ctx.enter_context(nc.sbuf_tensor("exsched", [1, 16], I32))
        bcast_t = ctx.enter_context(nc.sbuf_tensor("bcast_t", [128, 64], F32))

        rt8 = ctx.enter_context(nc.sbuf_tensor("rt8", [128, 6 * E], F32))
        rs1 = ctx.enter_context(nc.sbuf_tensor("rs1", [128, 8], F32))
        ixj = ctx.enter_context(nc.sbuf_tensor("ixj", [128, NT], I32))
        wr_sb = ctx.enter_context(nc.sbuf_tensor("wr_sb", [128, ND * E], F32))
        qkv_scope = ctx.enter_context(contextlib.ExitStack())
        qT = qkv_scope.enter_context(nc.sbuf_tensor("qT", [128, ND * S], BF16))
        kT = qkv_scope.enter_context(nc.sbuf_tensor("kT", [128, ND * S], BF16))
        v_sb = qkv_scope.enter_context(nc.sbuf_tensor("v_sb", [128, NT * NH * 65], BF16))
        acc_scope = ctx.enter_context(contextlib.ExitStack())
        acc = None

        block = ctx.enter_context(nc.Block())

        EMIT("y", lambda e: e.dma_start(out=id_f[:, :], in_=id_h), "dA", 16)
        EMIT("g", lambda e: e.dma_start(out=id_b[:, :], in_=id_h), "dC", 16)
        EMIT("g", lambda e: e.dma_start(out=tril_b[:, :], in_=tril_h), "dC", 16)
        EMIT("g", lambda e: e.dma_start(out=br_t[:, :], in_=br_h.partition_broadcast(128)), "dC", 16)
        EMIT("g", lambda e: e.dma_start(
            out=wr_sb[:, :].rearrange("p (t e) -> p t e", t=ND),
            in_=wr_h.rearrange("(t p) e -> p t e", p=128)), "dC", 16)
        vC = P.sem_val["dC"]
        EMIT("v", lambda e: e.memset(ones_b[:, :], 1.0), "tV")
        EMIT("v", lambda e: e.memset(ones8[:, :], 1.0), "tV")
        def _setup_bounds(e):
            r1 = e.alloc_register("bnd_lst")
            e.reg_mov(r1, NCH * 512 - 1)
            e._bnd_lst = RuntimeValue(val=r1)
            r2 = e.alloc_register("bnd_yslt")
            e.reg_mov(r2, S - 1)
            e._bnd_yslt = RuntimeValue(val=r2)
        P.add("g", _setup_bounds)
        EMIT("g", lambda e: e.iota(iota_p[:, :], pattern=[[1, 1]], base=0,
                                   channel_multiplier=1,
                                   allow_small_or_imprecise_dtypes=True))
        EMIT("g", lambda e: e.iota(iota_eb[:, :], pattern=[[1, E]], base=0,
                                   channel_multiplier=0,
                                   allow_small_or_imprecise_dtypes=True))
        EMIT("g", lambda e: e.iota(iok_b[:, :], pattern=[[1, 16]], base=0,
                                   channel_multiplier=0,
                                   allow_small_or_imprecise_dtypes=True))
        EMIT("g", lambda e: e.iota(ixj[:, :], pattern=[[128, NT]], base=0,
                                   channel_multiplier=1,
                                   allow_small_or_imprecise_dtypes=True))
        veps = EMIT("v", lambda e: e.memset(eps_t[:, :], 1e-5), "tV")
        EMIT("v", lambda e: e.memset(
            v_sb[:, :].rearrange("p (g c) -> p g c", c=65)[:, :, 64:65], 1.0), "tV")

        pre_ctr = [0]

        def emit_prestage(n):
            for _ in range(n):
                mm = pre_ctr[0]
                if mm >= 64:
                    return
                pre_ctr[0] += 1
                is_w1 = mm < 32
                qid = mm if is_w1 else mm - 32
                e_i2, q2 = qid // 4, qid % 4
                if is_w1:
                    EMIT("g", lambda e, e_i2=e_i2, q2=q2, qid=qid: e.dma_start(
                        out=w1bf_h.rearrange("(qq p) f2 -> qq p f2", p=128)[qid]
                        .rearrange("p (t ff) -> p t ff", ff=1024),
                        in_=w1_h[e_i2][:, 1024 * q2:1024 * (q2 + 1)].rearrange(
                            "(t p) ff -> p t ff", p=128)),
                        "dPa", 16)
                else:
                    EMIT("g", lambda e, e_i2=e_i2, q2=q2: e.dma_start(
                        out=w2bf_h.rearrange("(ee p) f2 -> ee p f2", p=128)[e_i2]
                        [:, q2 * 8192:(q2 + 1) * 8192].rearrange("p (t dd) -> p t dd", dd=1024),
                        in_=w2_h[e_i2].rearrange("(q t p) d -> q p t d", q=4, p=128)[q2]),
                        "dPa", 16)

        with contextlib.ExitStack() as a1:
            xnT = a1.enter_context(nc.sbuf_tensor("xnT", [128, ND * S], BF16))
            x_sb = a1.enter_context(nc.sbuf_tensor("x_sb", [128, 2 * D], F32))
            xnb = a1.enter_context(nc.sbuf_tensor("xnb", [128, 2 * D], BF16))
            bs_t = a1.enter_context(nc.sbuf_tensor("bs_t", [128, 2 * 12], F32))
            ba_t = a1.enter_context(nc.sbuf_tensor("ba_t", [128, 2 * 2], F32))
            st1 = a1.enter_context(nc.sbuf_tensor("st1", [128, 8], F32))
            w_sb = a1.enter_context(nc.sbuf_tensor("w_sb", [128, 3 * ND * D], BF16))

            xnTf = a1.enter_context(nc.sbuf_tensor("xnTf", [128, 1024], F32))
            pbR = a1.enter_context(nc.psum_tensor("pbR", [128, 1024], F32))
            pb0 = a1.enter_context(nc.psum_tensor("pb0", [128, 512], F32))
            pb1 = a1.enter_context(nc.psum_tensor("pb1", [128, 512], F32))
            pb2 = a1.enter_context(nc.psum_tensor("pb2", [128, 512], F32))

            for wi, wh in enumerate([wq_h, wk_h, wv_h]):
                EMIT("g", lambda e, wi=wi, wh=wh: e.dma_start(
                    out=w_sb[:, wi * ND * D:(wi + 1) * ND * D].rearrange("p (t f) -> p t f", t=ND),
                    in_=wh.rearrange("(t p) f -> p t f", p=128)), "dWq", 16)
            vwqkv = P.sem_val["dWq"]

            x_slot_free = [None, None]
            xnd_dumped = [None, None]
            last_xn_evac = None
            xnTf_free = [None]
            qkv_par = [0]

            def emit_qkv_chunk(sc, xn_ready):
                for wi, dst in ((0, qT), (1, kT)):
                    for u in range(ND):
                        par = qkv_par[0] % 2
                        qkv_par[0] += 1
                        pb = pb0 if par == 0 else pb1
                        pbn = "pb0" if par == 0 else "pb1"
                        W("t", "dWq", vwqkv)
                        W("t", "tV", xn_ready)
                        bank_wait("t", pbn)
                        def _mm(e, wi=wi, u=u, sc=sc, pb=pb):
                            ins = None
                            for d in range(ND):
                                ins = e.matmul(pb[:, :],
                                               w_sb[:, wi * ND * D + d * D + 128 * u:
                                                    wi * ND * D + d * D + 128 * (u + 1)],
                                               xnT[:, d * S + 512 * sc: d * S + 512 * (sc + 1)],
                                               start=(d == 0), stop=(d == ND - 1))
                            return ins
                        vmm = EMIT("t", _mm, "tT")
                        bank_done("t", pbn, "tT", vmm)
                        ekey = "s" if par == 0 else "v"
                        bank_wait(ekey, pbn)
                        if ekey == "s":
                            vev = EMIT("s", lambda e, u=u, sc=sc, pb=pb, dst=dst: e.activation(
                                dst[:, u * S + 512 * sc: u * S + 512 * (sc + 1)],
                                pb[:, :], AF.Copy), "tS")
                        else:
                            vev = EMIT("v", lambda e, u=u, sc=sc, pb=pb, dst=dst: e.tensor_copy(
                                dst[:, u * S + 512 * sc: u * S + 512 * (sc + 1)],
                                pb[:, :]), "tV")
                        bank_done(ekey, pbn, OWN[ekey], vev)
                for j2 in range(4 * sc, 4 * sc + 4):
                    for ch in range(2):
                        par = qkv_par[0] % 2
                        qkv_par[0] += 1
                        pb = pb0 if par == 0 else pb1
                        pbn = "pb0" if par == 0 else "pb1"
                        W("t", "dWq", vwqkv)
                        W("t", "tV", xn_ready)
                        bank_wait("t", pbn)
                        def _mmv(e, j2=j2, ch=ch, pb=pb):
                            ins = None
                            for d in range(ND):
                                ins = e.matmul(pb[:, :],
                                               xnT[:, d * S + 128 * j2: d * S + 128 * (j2 + 1)],
                                               w_sb[:, 2 * ND * D + d * D + 512 * ch:
                                                    2 * ND * D + d * D + 512 * (ch + 1)],
                                               start=(d == 0), stop=(d == ND - 1))
                            return ins
                        vmm = EMIT("t", _mmv, "tT")
                        bank_done("t", pbn, "tT", vmm)
                        ekey = "s" if ch == 0 else "v"
                        bank_wait(ekey, pbn)
                        base = j2 * NH * 65 + ch * 8 * 65
                        if ekey == "s":
                            vev = EMIT("s", lambda e, base=base, pb=pb: e.activation(
                                v_sb[:, base:base + 8 * 65].rearrange("p (h c) -> p h c", c=65)[:, :, 0:64],
                                pb[:, :].rearrange("p (h c) -> p h c", c=64), AF.Copy), "tS")
                        else:
                            vev = EMIT("v", lambda e, base=base, pb=pb: e.tensor_copy(
                                v_sb[:, base:base + 8 * 65].rearrange("p (h c) -> p h c", c=65)[:, :, 0:64],
                                pb[:, :].rearrange("p (h c) -> p h c", c=64)), "tV")
                        bank_done(ekey, pbn, OWN[ekey], vev)

            for j in range(NT):
                slot = j % 2
                dxs = "dX0" if slot == 0 else "dX1"
                if x_slot_free[slot] is not None:
                    for sname, val in x_slot_free[slot]:
                        W("y", sname, val)
                vload = EMIT("y", lambda e, j=j, slot=slot: e.dma_start(
                    out=x_sb[:, slot * D:(slot + 1) * D],
                    in_=x_h.rearrange("(j p) d -> j p d", p=128)[j]), dxs, 16)
                W("v", dxs, vload)
                so = slot * 12
                sa = slot * 2
                EMIT("v", lambda e, slot=slot, so=so: e.bn_stats(
                    bs_t[:, so:so + 6], x_sb[:, slot * D: slot * D + 512]))
                EMIT("v", lambda e, slot=slot, so=so: e.bn_stats(
                    bs_t[:, so + 6:so + 12], x_sb[:, slot * D + 512:(slot + 1) * D]))
                DR("v")
                vvar = EMIT("v", lambda e, so=so, sa=sa: e.bn_aggr(
                    ba_t[:, sa:sa + 2], bs_t[:, so:so + 12]), "tV")
                W("s", "tV", vvar)
                EMIT("s", lambda e, sa=sa, slot=slot: e.activation(
                    st1[:, 2 + slot:3 + slot], ba_t[:, sa + 1:sa + 2], AF.Ln,
                    bias=eps_t[:, 0:1], scale=1.0))
                DR("s")
                vrstd = EMIT("s", lambda e, slot=slot: e.activation(
                    st1[:, 4 + slot:5 + slot], st1[:, 2 + slot:3 + slot], AF.Exp,
                    scale=-0.5), "tS")
                W("v", "tS", vrstd)
                DR("v")
                vxn = EMIT("v", lambda e, slot=slot, sa=sa: e.tensor_scalar(
                    x_sb[:, slot * D:(slot + 1) * D], x_sb[:, slot * D:(slot + 1) * D],
                    ba_t[:, sa:sa + 1], st1[:, 4 + slot:5 + slot],
                    op0=ALU.subtract, op1=ALU.mult), "tV")
                W("s", "tV", vxn)
                if xnd_dumped[slot] is not None:
                    W("s", "dXd%d" % slot, xnd_dumped[slot])
                vxnb = EMIT("s", lambda e, slot=slot: e.activation(
                    xnb[:, slot * D:(slot + 1) * D],
                    x_sb[:, slot * D:(slot + 1) * D], AF.Copy), "tS")
                W("y", "tS", vxnb)
                xnd_dumped[slot] = EMIT("y", lambda e, j=j, slot=slot: e.dma_start(
                    out=xnd_h.rearrange("(j p) d -> j p d", p=128)[j],
                    in_=xnb[:, slot * D:(slot + 1) * D]), "dXd%d" % slot, 16)
                W("t", "tV", vxn)
                W("t", "dC", vC)
                if j == 0:
                    W("t", "dA", 16)
                bank_wait("t", "pbR")
                def _tr(e, j=j, slot=slot):
                    ins = None
                    for d in range(ND):
                        ins = e.transpose(pbR[:, d * 128:(d + 1) * 128],
                                          x_sb[:, slot * D + d * 128: slot * D + (d + 1) * 128],
                                          id_f[:, :])
                    return ins
                vtr = EMIT("t", _tr, "tT")
                x_slot_free[slot] = [("tS", vxnb), ("tT", vtr)]
                bank_done("t", "pbR", "tT", vtr)
                emit_prestage(2)
                bank_wait("v", "pbR")
                def _ev(e, j=j):
                    ins = None
                    for d in range(ND):
                        ins = e.tensor_copy(xnT[:, d * S + 128 * j: d * S + 128 * (j + 1)],
                                            pbR[:, d * 128:(d + 1) * 128])
                    return ins
                vev = EMIT("v", _ev, "tV")
                last_xn_evac = vev
                if xnTf_free[0] is not None:
                    W("v", xnTf_free[0][0], xnTf_free[0][1])
                DR("v")
                vef = EMIT("v", lambda e: e.tensor_copy(xnTf[:, :], pbR[:, 0:1024]), "tV")
                bank_done("v", "pbR", "tV", vef)

                W("t", "tV", vef)
                bank_wait("t", "pb2")
                def _rt(e):
                    ins = None
                    for d in range(ND):
                        ins = e.matmul(pb2[:, 0:E],
                                       xnTf[:, d * 128:(d + 1) * 128],
                                       wr_sb[:, d * E:(d + 1) * E],
                                       start=(d == 0), stop=(d == ND - 1))
                    return ins
                vrt = EMIT("t", _rt, "tT")
                xnTf_free[0] = ("tT", vrt)
                bank_done("t", "pb2", "tT", vrt)
                bank_wait("v", "pb2")
                vpb2 = EMIT("v", lambda e, j=j: e.tensor_add(
                    lg_all[:, j * E:(j + 1) * E], pb2[:, 0:E], br_t[:, :]), "tV")
                bank_done("v", "pb2", "tV", vpb2)
                DR("v")
                if j % 4 == 3:
                    emit_qkv_chunk(j // 4, last_xn_evac)

            for ek in ["t", "s", "v"]:
                W(ek, OWN[ek], P.sem_val[OWN[ek]])
                P.add(ek, lambda e: e.nop().then_inc(sem["ph"], 1))
                P.inc("ph")
            W("y", "dA", P.sem_val["dA"])
            W("y", "dX0", P.sem_val["dX0"])
            W("y", "dX1", P.sem_val["dX1"])
            W("y", "dXd0", P.sem_val["dXd0"])
            W("y", "dXd1", P.sem_val["dXd1"])
            P.add("y", lambda e: e.nop().then_inc(sem["ph"], 1))
            P.inc("ph")
            W("g", "dWq", P.sem_val["dWq"])
            W("g", "dC", P.sem_val["dC"])
            P.add("g", lambda e: e.nop().then_inc(sem["ph"], 1))
            bar_a1 = P.inc("ph")
        for ek in ["t", "s", "v", "g", "y"]:
            W(ek, "ph", bar_a1)
        bank_last.clear()
        acc = acc_scope.enter_context(nc.sbuf_tensor("acc", [128, NT * D], BF16, side="right"))

        rt = contextlib.ExitStack()
        sel_ch = rt.enter_context(nc.sbuf_tensor("sel_ch", [8, 512], F32))
        cum_ch = rt.enter_context(nc.sbuf_tensor("cum_ch", [8, 512], F32))
        pos_ch = rt.enter_context(nc.sbuf_tensor("pos_ch", [8, 512], F32))
        zer_ch = rt.enter_context(nc.sbuf_tensor("zer_ch", [8, 512], F32))
        carry = rt.enter_context(nc.sbuf_tensor("carry", [8, 4], F32))
        pos_tm = rt.enter_context(nc.sbuf_tensor("pos_tm", [128, NT * E], F32))
        nch_c = rt.enter_context(nc.sbuf_tensor("nch_c", [8, 4], F32))
        srow = rt.enter_context(nc.sbuf_tensor("srow", [1, 64], F32))
        mrow = rt.enter_context(nc.sbuf_tensor("mrow", [8, 64], F32))
        mrow_b = rt.enter_context(nc.sbuf_tensor("mrow_b", [8, 64], BF16))
        lsi = rt.enter_context(nc.sbuf_tensor("lsi", [128, 132 * 8], F32))
        sci32 = [rt.enter_context(nc.sbuf_tensor("sci%d" % i, [128, 8], I32)) for i in range(32)]
        sct4 = [rt.enter_context(nc.sbuf_tensor("sct%d" % i, [128, 8], F32)) for i in range(32)]
        rsc = rt.enter_context(nc.sbuf_tensor("rsc", [128, 8], F32))
        pR = rt.enter_context(nc.psum_tensor("pR", [128, 512], F32))
        scat_done = [None, None]

        def routing_steps():
            gates_done_tv = P.sem_val["tV"]
            lsi3 = lsi[:, :].rearrange("p (c three) -> p c three", three=8)
            EMIT("v", lambda e: e.memset(lsi3[:, :, :], 0.0))
            DR("v")
            vli = EMIT("v", lambda e: e.memset(lsi3[:, :, 1:2], HUGE), "tV")
            W("y", "tV", vli)
            EMIT("y", lambda e: e.dma_start(
                out=lst_h[0:NCH * 512].rearrange("(c p) eight -> p c eight", p=128),
                in_=lsi3), "dOut", 16)
            vini = EMIT("y", lambda e: e.dma_start(
                out=lst_h[NCH * 512:NCH * 512 + 1],
                in_=lsi[0:1, 0:8]), "dOut", 16)
            EMIT("v", lambda e: e.memset(zer_ch[:, :], 0.0))
            EMIT("v", lambda e: e.memset(carry[:, :], 0.0))
            DR("v")
            yield
            for c4 in range(4):
                W("t", "tV", gates_done_tv)
                bank_wait("t", "pR")
                def _trs(e, c4=c4):
                    ins = None
                    for i in range(4):
                        j = 4 * c4 + i
                        ins = e.transpose(pR[0:8, 128 * i:128 * (i + 1)],
                                          ind2_all[:, j * E:(j + 1) * E], id_f[:, :])
                    return ins
                vt1 = EMIT("t", _trs, "tT")
                bank_done("t", "pR", "tT", vt1)
                bank_wait("v", "pR")
                EMIT("v", lambda e: e.tensor_copy(sel_ch[:, :], pR[0:8, 0:512]))
                DR("v")
                init_ap = 0.0 if c4 == 0 else carry[:, c4 - 1:c4]
                EMIT("v", lambda e, init_ap=init_ap: e.tensor_tensor_scan(
                    cum_ch[:, :], sel_ch[:, :], zer_ch[:, :], init_ap,
                    op0=ALU.add, op1=ALU.add))
                DR("v")
                EMIT("v", lambda e: e.tensor_sub(pos_ch[:, :], cum_ch[:, :], sel_ch[:, :]))
                DR("v")
                vcar = EMIT("v", lambda e, c4=c4: e.tensor_copy(
                    carry[:, c4:c4 + 1], cum_ch[:, 511:512]), "tV")
                bank_done("v", "pR", "tV", vcar)
                DR("v")
                yield
                W("t", "tV", vcar)
                bank_wait("t", "pR")
                def _trp(e, c4=c4):
                    ins = None
                    for i in range(4):
                        ins = e.transpose(pR[:, 32 * i:32 * i + 8],
                                          pos_ch[0:8, 128 * i:128 * (i + 1)],
                                          id_f[0:8, 0:8])
                    return ins
                vt2 = EMIT("t", _trp, "tT")
                bank_done("t", "pR", "tT", vt2)
                bank_wait("v", "pR")
                def _evp(e, c4=c4):
                    ins = None
                    for i in range(4):
                        j = 4 * c4 + i
                        ins = e.tensor_copy(pos_tm[:, j * E:(j + 1) * E],
                                            pR[:, 32 * i:32 * i + 8])
                    return ins
                vev = EMIT("v", _evp, "tV")
                bank_done("v", "pR", "tV", vev)
                DR("v")
                yield
            for c in range(4):
                EMIT("v", lambda e, c=c: e.tensor_scalar(
                    nch_c[:, c:c + 1], carry[:, 3:4], -(512.0 * c + 1.0), 0.0,
                    op0=ALU.add, op1=ALU.is_ge))
                DR("v")
            vnch = EMIT("v", lambda e: e.reduce_sum(rsc[0:8, 0:1], nch_c[:, 0:4], axis=AX.X), "tV")
            DR("v")
            W("t", "tV", vnch)
            bank_wait("t", "pR")
            vtn = EMIT("t", lambda e: e.transpose(pR[0:1, 0:8], rsc[0:8, 0:1],
                                                  id_f[0:8, 0:8]), "tT")
            bank_done("t", "pR", "tT", vtn)
            bank_wait("v", "pR")
            EMIT("v", lambda e: e.tensor_copy(srow[0:1, 0:8], pR[0:1, 0:8]))
            DR("v")
            yield
            EMIT("v", lambda e: e.tensor_tensor_scan(
                srow[0:1, 8:16], srow[0:1, 0:8], srow[0:1, 0:8], 0.0,
                op0=ALU.add, op1=ALU.bypass))
            DR("v")
            vcr = EMIT("v", lambda e: e.engine_nop() if False else e.tensor_copy(
                srow[0:1, 16:24], srow[0:1, 8:16]), "tV")
            bank_done("v", "pR", "tV", vcr)
            W("t", "tV", vcr)
            bank_wait("t", "pR")
            vtc = EMIT("t", lambda e: e.transpose(pR[0:8, 0:1], srow[0:1, 8:16],
                                                  id_f[0:1, 0:1]), "tT")
            bank_done("t", "pR", "tT", vtc)
            bank_wait("v", "pR")
            EMIT("v", lambda e: e.tensor_copy(nch_c[:, 1:2], pR[0:8, 0:1]))
            DR("v")
            yield
            EMIT("v", lambda e: e.tensor_scalar(
                mrow[:, 0:16], iok_b[:, :], nch_c[:, 1:2], 0.0,
                op0=ALU.subtract, op1=ALU.is_ge))
            DR("v")
            EMIT("v", lambda e: e.tensor_scalar_mul(mrow[:, 16:32], mrow[:, 0:16],
                                                    rsc[0:8, 0:1]))
            DR("v")
            vmb = EMIT("v", lambda e: e.tensor_copy(mrow_b[:, 0:32], mrow[:, 0:32]), "tV")
            bank_done("v", "pR", "tV", vmb)
            yield
            W("t", "tV", vmb)
            bank_wait("t", "pR")
            vms = EMIT("t", lambda e: e.matmul(pR[0:1, 0:32], ones8[:, 0:1], mrow_b[:, 0:32],
                                               start=True, stop=True), "tT")
            bank_done("t", "pR", "tT", vms)
            bank_wait("v", "pR")
            EMIT("v", lambda e: e.tensor_copy(srow[0:1, 32:48], pR[0:1, 0:16]))
            DR("v")
            EMIT("v", lambda e: e.tensor_sub(srow[0:1, 48:64], iok_b[0:1, :],
                                             pR[0:1, 16:32]))
            DR("v")
            vsm = EMIT("v", lambda e: e.engine_nop() if False else e.tensor_copy(
                srow[0:1, 16:24], srow[0:1, 8:16]), "tV")
            bank_done("v", "pR", "tV", vsm)
            DR("v")
            EMIT("v", lambda e: e.tensor_scalar_min(srow[0:1, 0:16], srow[0:1, 32:48], 7.0))
            DR("v")
            EMIT("v", lambda e: e.scalar_tensor_tensor(
                srow[0:1, 16:32], srow[0:1, 32:48], 4.0, srow[0:1, 48:64],
                op0=ALU.mult, op1=ALU.add))
            DR("v")
            EMIT("v", lambda e: e.tensor_scalar_min(srow[0:1, 16:32], srow[0:1, 16:32], 32.0))
            DR("v")
            EMIT("v", lambda e: e.tensor_copy(exsched[0:1, :], srow[0:1, 0:16]))
            DR("v")
            yield
            EMIT("v", lambda e: e.tensor_scalar_mul(mrow[0:1, 0:16], srow[0:1, 0:16], 512.0))
            DR("v")
            EMIT("v", lambda e: e.tensor_scalar_mul(mrow[0:1, 16:32], srow[0:1, 16:32], 512.0))
            DR("v")
            EMIT("v", lambda e: e.tensor_scalar_mul(mrow[0:1, 32:48], srow[0:1, 0:16], 32.0))
            DR("v")
            EMIT("v", lambda e: e.tensor_scalar_mul(mrow[0:1, 48:64], srow[0:1, 0:16], 128.0))
            DR("v")
            vbr = EMIT("v", lambda e: e.tensor_copy(mrow_b[0:1, 0:64], mrow[0:1, 0:64]), "tV")
            W("t", "tV", vbr)
            bank_wait("t", "pR")
            vbm = EMIT("t", lambda e: e.matmul(pR[:, 0:64], ones_b[0:1, 0:128],
                                               mrow_b[0:1, 0:64], start=True, stop=True), "tT")
            bank_done("t", "pR", "tT", vbm)
            bank_wait("v", "pR")
            EMIT("v", lambda e: e.tensor_copy(bcast_t[:, :], pR[:, 0:64]))
            DR("v")
            yield
            for i4 in range(32):
                EMIT("v", lambda e, i4=i4: e.memset(sct4[i4][:, :], 0.0))
                EMIT("v", lambda e, i4=i4: e.memset(sci32[i4][:, :], 0))
            DR("v")
            yield
            W("g", "dOut", vini)
            for j in range(NT):
                par = j % 2
                dyn = "dY%d" % par
                EMIT("v", lambda e, j=j: e.tensor_sub(
                    rsc[:, 0:8], ind2_all[:, j * E:(j + 1) * E],
                    ind_all[:, j * E:(j + 1) * E]))
                DR("v")
                for r in range(2):
                    mask = (lambda jj: ind_all[:, jj * E:(jj + 1) * E])(j) if r == 0 else rsc[:, 0:8]
                    EMIT("v", lambda e, mask=mask: e.tensor_mul(rsc[:, 8:16] if False else rt8[:, 0:8], mask, iota_eb[:, :]))
                    DR("v")
                    EMIT("v", lambda e: e.reduce_sum(rs1[:, 0:1], rt8[:, 0:8], axis=AX.X))
                    DR("v")
                    EMIT("v", lambda e, mask=mask, j=j: e.tensor_mul(
                        rt8[:, 0:8], mask, pos_tm[:, j * E:(j + 1) * E]))
                    DR("v")
                    EMIT("v", lambda e: e.reduce_sum(rs1[:, 1:2], rt8[:, 0:8], axis=AX.X))
                    DR("v")
                    EMIT("v", lambda e, mask=mask, j=j: e.tensor_mul(
                        rt8[:, 0:8], mask, gates[:, j * E:(j + 1) * E]))
                    DR("v")
                    EMIT("v", lambda e, j=j, r=r: e.reduce_sum(sct4[j * 2 + r][:, 2:3],
                                                               rt8[:, 0:8], axis=AX.X))
                    DR("v")
                    EMIT("v", lambda e, j=j, r=r: e.tensor_scalar_add(
                        sct4[j * 2 + r][:, 0:1], iota_p[:, :], float(128 * j)))
                    DR("v")
                    EMIT("v", lambda e, j=j, r=r: e.tensor_scalar_add(
                        sct4[j * 2 + r][:, 1:2], iota_p[:, :],
                        float(128 * j)))
                    DR("v")
                    EMIT("v", lambda e: e.scalar_tensor_tensor(
                        rs1[:, 2:3], rs1[:, 0:1], 2048.0, rs1[:, 1:2],
                        op0=ALU.mult, op1=ALU.add))
                    DR("v")
                    EMIT("v", lambda e: e.tensor_scalar_min(
                        rs1[:, 2:3], rs1[:, 2:3], float(NCH * 512)))
                    DR("v")
                    vsc = EMIT("v", lambda e, j=j, r=r: e.tensor_copy(sci32[j * 2 + r][:, 0:1],
                                                                      rs1[:, 2:3]), "tV")
                    DR("v")
                    W("g", "tV", vsc)
                    scat_done[par] = EMIT("g", lambda e, j=j, r=r: e.indirect_dma_start(
                        out=lst_h,
                        out_offset=bass.IndirectOffsetOnAxis(ap=sci32[j * 2 + r][:, 0:1], axis=0),
                        in_=sct4[j * 2 + r][:, :], in_offset=None,
                        bounds_check=e._bnd_lst, oob_is_err=False), dyn, 16)
                    yield
        rgen = routing_steps()

        with contextlib.ExitStack() as a2:
            pT = a2.enter_context(nc.sbuf_tensor("pT", [128, 16 * 512], BF16))
            ctx_all = a2.enter_context(nc.sbuf_tensor("ctx_all", [128, 8 * 512], BF16))
            wo_sb = a2.enter_context(nc.sbuf_tensor("wo_sb", [128, 8 * 1024], BF16))
            rrow_f = a2.enter_context(nc.sbuf_tensor("rrow_f", [1, 512], BF16))
            rrow_b = a2.enter_context(nc.sbuf_tensor("rrow_b", [1, 512], BF16))
            bc_sb = a2.enter_context(nc.sbuf_tensor("bc_sb", [64, 512], BF16))

            pL = [a2.enter_context(nc.psum_tensor("pL%d" % i, [128, 512], F32))
                  for i in range(4)]
            pC = [a2.enter_context(nc.psum_tensor("pC0", [65, 512], F32)),
                  a2.enter_context(nc.psum_tensor("pC1", [65, 512], F32))]
            pB = a2.enter_context(nc.psum_tensor("pB", [64, 512], F32))
            pO = [pL[0], pL[1]]

            vwo = EMIT("g", lambda e: e.dma_start(
                out=wo_sb[:, :].rearrange("p (r d) -> p r d", r=8),
                in_=wo_h.rearrange("(r p) d -> p r d", p=128)), "dWo", 16)

            lctr = 0
            pv_reads = {}
            gate_vals = {}
            wo_waited = False

            def emit_gates(j):
                lg = lg_all[:, j * E:(j + 1) * E]
                eg = rt8[:, 1 * E:2 * E]
                ind = ind_all[:, j * E:(j + 1) * E]
                ind2 = ind2_all[:, j * E:(j + 1) * E]
                u = rt8[:, 4 * E:5 * E]
                p_ = rt8[:, 5 * E:6 * E]
                DR("v")
                EMIT("v", lambda e, lg=lg: e.reduce_max(rs1[:, 0:1], lg, axis=AX.X, negate=True))
                DR("v")
                EMIT("v", lambda e, lg=lg, ind=ind: e.tensor_scalar(
                    ind, lg, rs1[:, 0:1], 0.0, op0=ALU.add, op1=ALU.is_ge))
                DR("v")
                EMIT("v", lambda e, lg=lg, ind=ind, u=u: e.scalar_tensor_tensor(
                    u, ind, -1e30, lg, op0=ALU.mult, op1=ALU.add))
                DR("v")
                vnm2 = EMIT("v", lambda e, u=u: e.reduce_max(
                    rs1[:, 1:2], u, axis=AX.X, negate=True), "tV")
                W("s", "tV", vnm2)
                DR("s")
                vexp = EMIT("s", lambda e, lg=lg, eg=eg: e.activation(
                    eg, lg, AF.Exp, bias=rs1[:, 0:1], scale=1.0), "tS")
                W("v", "tS", vexp)
                EMIT("v", lambda e, lg=lg, ind2=ind2: e.tensor_scalar(
                    ind2, lg, rs1[:, 1:2], 0.0, op0=ALU.add, op1=ALU.is_ge))
                DR("v")
                EMIT("v", lambda e, eg=eg, ind2=ind2, p_=p_: e.tensor_mul(p_, eg, ind2))
                DR("v")
                EMIT("v", lambda e, p_=p_: e.reduce_sum(rs1[:, 2:3], p_, axis=AX.X))
                DR("v")
                EMIT("v", lambda e: e.reciprocal(rs1[:, 3:4], rs1[:, 2:3]))
                DR("v")
                EMIT("v", lambda e, j=j, p_=p_: e.tensor_scalar_mul(
                    gates[:, j * E:(j + 1) * E], p_, rs1[:, 3:4]), "tV")
                DR("v")

            DEPTH = 3
            pending_norm = [None]
            last_vbc = [None]
            for qc in range(NQC):
                ctx_ready = []
                for h in range(NH):
                    hp = h % 2
                    hb = h // 2
                    cpn = "pC%d" % hp
                    nt_ = 4 * qc + 4
                    soff = 8 * hp if nt_ <= 8 else 0
                    for i in range(nt_ + DEPTH):
                        if i < nt_:
                            t = i
                            sl = soff + t
                            qv0 = max(0, 128 * t - 512 * qc)
                            lb = lctr % 4
                            lctr += 1
                            lbn = "pL%d" % lb
                            bank_wait("t", lbn)
                            vlg = EMIT("t", lambda e, t=t, qc=qc, hp=hp, hb=hb, qv0=qv0, lb=lb:
                                       e.matmul(pL[lb][:, qv0:512],
                                                kT[64 * hp:64 * hp + 64,
                                                   hb * S + 128 * t: hb * S + 128 * (t + 1)],
                                                qT[64 * hp:64 * hp + 64,
                                                   hb * S + 512 * qc + qv0: hb * S + 512 * (qc + 1)],
                                                start=True, stop=True), "tT")
                            bank_done("t", lbn, "tT", vlg)
                            bank_wait("s", lbn)
                            if sl in pv_reads:
                                W("s", "tT", pv_reads[sl])
                            vex = EMIT("s", lambda e, sl=sl, qv0=qv0, lb=lb: e.activation(
                                pT[:, 512 * sl + qv0: 512 * (sl + 1)],
                                pL[lb][:, qv0:512], AF.Exp, scale=0.125), "tS")
                            bank_done("s", lbn, "tS", vex)
                            gate_vals[t] = ("tS", vex)
                            if t >= 4 * qc:
                                W("v", "tS", vex)
                                vmask = EMIT("v", lambda e, sl=sl, qv0=qv0: e.tensor_mul(
                                    pT[:, 512 * sl + qv0: 512 * sl + qv0 + 128],
                                    pT[:, 512 * sl + qv0: 512 * sl + qv0 + 128],
                                    tril_b[:, :]), "tV")
                                DR("v")
                                gate_vals[t] = ("tV", vmask)
                        if i == 1 and pending_norm[0] is not None:
                            pending_norm[0]()
                            pending_norm[0] = None
                        if i >= DEPTH:
                            t = i - DEPTH
                            sl = soff + t
                            qv0 = max(0, 128 * t - 512 * qc)
                            gv = gate_vals[t]
                            W("t", gv[0], gv[1])
                            if t == 0:
                                bank_wait("t", cpn)
                            vpv = EMIT("t", lambda e, t=t, sl=sl, h=h, hp=hp, qv0=qv0,
                                       first=(t == 0), last=(t == nt_ - 1):
                                       e.matmul(pC[hp][:, qv0:512],
                                                v_sb[:, t * NH * 65 + 65 * h: t * NH * 65 + 65 * h + 65],
                                                pT[:, 512 * sl + qv0: 512 * (sl + 1)],
                                                start=first, stop=last), "tT")
                            pv_reads[sl] = vpv
                    bank_done("t", cpn, "tT", vpv)
                    bank_wait("s", cpn)
                    if last_vbc[0] is not None:
                        W("s", "tT", last_vbc[0])
                    DR("s")
                    EMIT("s", lambda e, hp=hp: e.activation(rrow_f[:, :], pC[hp][64:65, :], AF.Ln))
                    DR("s")
                    vrb = EMIT("s", lambda e: e.activation(rrow_b[:, :], rrow_f[:, :], AF.Exp,
                                                           scale=-1.0), "tS")
                    bank_done("s", cpn, "tS", vrb)

                    def _norm_tail(hp=hp, hb=hb, cpn=cpn, vrb=vrb):
                        W("t", "tS", vrb)
                        bank_wait("t", "pB")
                        vbc = EMIT("t", lambda e: e.matmul(pB[0:64, :], ones_b[:, 0:64], rrow_b[:, :],
                                                           start=True, stop=True), "tT")
                        last_vbc[0] = vbc
                        bank_done("t", "pB", "tT", vbc)
                        bank_wait("v", "pB")
                        EMIT("v", lambda e: e.tensor_copy(bc_sb[:, :], pB[0:64, :]))
                        DR("v")
                        vnz = EMIT("v", lambda e, hp=hp, hb=hb: e.tensor_mul(
                            ctx_all[64 * hp:64 * hp + 64, 512 * hb: 512 * (hb + 1)],
                            pC[hp][0:64, :], bc_sb[:, :]), "tV")
                        DR("v")
                        bank_done("v", "pB", "tV", vnz)
                        bank_done("v", cpn, "tV", vnz)
                        ctx_ready.append(vnz)
                    pending_norm[0] = _norm_tail
                    if qc == 0:
                        emit_gates(h)
                    if qc >= 1:
                        next(rgen, None)
                if pending_norm[0] is not None:
                    pending_norm[0]()
                    pending_norm[0] = None
                if not wo_waited:
                    W("t", "dWo", vwo)
                    wo_waited = True
                for half in range(2):
                    for j4 in range(4):
                        bank_wait("t", "pL0")
                        bank_wait("t", "pL1")
                        W("t", "tV", ctx_ready[-1])
                        def _op(e, j4=j4, half=half):
                            ins = None
                            for h in range(NH):
                                hp = h % 2
                                hb = h // 2
                                ins = e.matmul(
                                    pO[hp][:, :],
                                    ctx_all[64 * hp:64 * hp + 64,
                                            512 * hb + 128 * j4: 512 * hb + 128 * (j4 + 1)],
                                    wo_sb[64 * hp:64 * hp + 64,
                                          1024 * hb + 512 * half: 1024 * hb + 512 * (half + 1)],
                                    start=(h < 2), stop=(h >= NH - 2))
                            return ins
                        vop = EMIT("t", _op, "tT")
                        bank_done("t", "pL0", "tT", vop)
                        bank_done("t", "pL1", "tT", vop)
                        bank_wait("v", "pL0")
                        bank_wait("v", "pL1")
                        jg = 4 * qc + j4
                        EMIT("v", lambda e, jg=jg, half=half: e.tensor_copy(
                            acc[:, jg * D + 512 * half: jg * D + 512 * (half + 1)], pO[0][:, :]))
                        DR("v")
                        voe = EMIT("v", lambda e, jg=jg, half=half: e.tensor_add(
                            acc[:, jg * D + 512 * half: jg * D + 512 * (half + 1)],
                            acc[:, jg * D + 512 * half: jg * D + 512 * (half + 1)],
                            pO[1][:, :]), "tV")
                        DR("v")
                        bank_done("v", "pL0", "tV", voe)
                        bank_done("v", "pL1", "tV", voe)
            for _ in rgen:
                pass
            emit_prestage(64)
            W("g", "tV", P.sem_val["tV"])
            EMIT("g", lambda e: e.dma_start(
                out=out_h.rearrange("(j p) d -> p j d", p=128),
                in_=acc[:, :].rearrange("p (j d) -> p j d", d=D)), "dPoB", 16)
            for ek in ["t", "s", "v"]:
                W(ek, OWN[ek], P.sem_val[OWN[ek]])
                P.add(ek, lambda e: e.nop().then_inc(sem["ph"], 1))
                P.inc("ph")
            W("y", "dA", P.sem_val["dA"])
            P.add("y", lambda e: e.nop().then_inc(sem["ph"], 1))
            P.inc("ph")
            W("g", "dWo", P.sem_val["dWo"])
            W("g", "dPa", P.sem_val.get("dPa", 0))
            W("g", "dY0", P.sem_val.get("dY0", 0))
            W("g", "dY1", P.sem_val.get("dY1", 0))
            W("g", "dOut", P.sem_val["dOut"])
            W("g", "dPoB", P.sem_val.get("dPoB", 0))
            P.add("g", lambda e: e.nop().then_inc(sem["ph"], 1))
            bar_a2 = P.inc("ph")
        for ek in ["t", "s", "v", "g", "y"]:
            W(ek, "ph", bar_a2)
        bank_last.clear()
        rt.close()
        acc_scope.close()
        qkv_scope.close()

        with contextlib.ExitStack() as f:
            ls8 = [f.enter_context(nc.sbuf_tensor("ls%d" % i, [128, 8], F32)) for i in range(8)]
            tok_i = f.enter_context(nc.sbuf_tensor("tok_i", [128, 2 * 4], I32))
            dst_i = f.enter_context(nc.sbuf_tensor("dst_i", [128, 2 * 4], I32))
            wix = f.enter_context(nc.sbuf_tensor("wix", [128, 2 * 5], I32))
            wixf = f.enter_context(nc.sbuf_tensor("wixf", [128, 5], F32))
            lsix = f.enter_context(nc.sbuf_tensor("lsix", [128, 2 * 4], I32))
            b1ix = f.enter_context(nc.sbuf_tensor("b1ix", [32, 2], I32))
            ixf = f.enter_context(nc.sbuf_tensor("ixf", [128, 4], F32))
            b2f = f.enter_context(nc.sbuf_tensor("b2f", [2, D], F32))
            b2ix = f.enter_context(nc.sbuf_tensor("b2ix", [2, 1], I32))
            xg4 = [f.enter_context(nc.sbuf_tensor("xg%d" % i, [128, 1024], BF16)) for i in range(4)]
            xgT = f.enter_context(nc.sbuf_tensor("xgT", [128, 2 * ND * 512], BF16))
            hT = f.enter_context(nc.sbuf_tensor("hT", [128, 32 * 512], BF16))
            w1r2 = [f.enter_context(nc.sbuf_tensor("w1r%d" % i, [128, ND * 1024], BF16)) for i in range(2)]
            w2full = f.enter_context(nc.sbuf_tensor("w2full", [128, 32 * D], BF16))
            b1st2 = [f.enter_context(nc.sbuf_tensor("b1st%d" % i, [32, 128], F32)) for i in range(2)]
            b1sb = f.enter_context(nc.sbuf_tensor("b1sb", [128, 2 * 32], F32))
            b2r = f.enter_context(nc.sbuf_tensor("b2r", [1, 2 * D], BF16))

            ysb2 = [f.enter_context(nc.sbuf_tensor("ysb%d" % i, [128, D], F32)) for i in range(2)]
            xr2 = [f.enter_context(nc.sbuf_tensor("xr%d" % i, [128, D], F32))
                   for i in range(2)]
            pH = [f.enter_context(nc.psum_tensor("pH%d" % i, [128, 512], F32)) for i in range(4)]
            pbT2 = f.enter_context(nc.psum_tensor("pbT2", [128, 1024], BF16))

            xr_free = [None, None]
            for jg in range(NT):
                slot = jg % 2
                dxs = "dX0" if slot == 0 else "dX1"
                if xr_free[slot] is not None:
                    W("y", "dPoA", xr_free[slot])
                vxl = EMIT("y", lambda e, jg=jg, slot=slot: e.dma_start(
                    out=xr2[slot][:, :],
                    in_=x_h.rearrange("(j p) d -> j p d", p=128)[jg]), dxs, 16)
                W("g", dxs, vxl)
                if jg == 0:
                    W("g", "dPoB", P.sem_val["dPoB"])
                else:
                    W("g", "dPoA", P.sem_val["dPoA"])
                xadd = EMIT("g", lambda e, jg=jg, slot=slot: e.indirect_dma_start(
                    out=out_h,
                    out_offset=bass.IndirectOffsetOnAxis(ap=ixj[:, jg:jg + 1], axis=0),
                    in_=xr2[slot][:, :], in_offset=None,
                    compute_op=ALU.add), "dPoA", 16)
                xr_free[slot] = xadd

            ls_free = [None, None]
            b2f_free = [None]
            xg_free = [None] * 4
            xgT_free = [None, None]
            w1_free = [None, None]
            w2_free = [None, None]
            ysb_free = [None, None]
            ls_n = [0, 0]
            g_n = [0, 0]
            w1_n = [0, 0]
            w2_n = [0, 0]
            w1ctr_l = [0]
            w2ctr = 0
            y_n = {"dY0": P.sem_val.get("dY0", 0) // 16, "dY1": P.sem_val.get("dY1", 0) // 16}
            rt_tv = P.sem_val["tV"]
            xadds_waited = [False]

            b1_ready_l = [None, None]
            b1st_free = [None, None]
            b1sb_free = [None, None]
            w1q_infos = [None, None]
            load_w1q_l = [None, None]

            def front(k):
                kp = k % 2
                dls = "dLs%d" % kp
                dg = "dG%d" % kp
                if ls_free[kp] is not None:
                    W("v", ls_free[kp][0], ls_free[kp][1])
                if k < 2:
                    pass
                DR("v")
                for c in range(4):
                    EMIT("v", lambda e, k=k, c=c: e.tensor_scalar(
                        ixf[:, c:c + 1], iota_p[:, :], bcast_t[:, 16 + k:17 + k],
                        float(128 * c), op0=ALU.add, op1=ALU.add))
                DR("v")
                EMIT("v", lambda e, kp=kp: e.tensor_copy(lsix[:, kp * 4:(kp + 1) * 4], ixf[:, 0:4]))
                DR("v")
                for c in range(4):
                    EMIT("v", lambda e, k=k, c=c: e.tensor_scalar(
                        wixf[:, c:c + 1], iota_p[:, :], bcast_t[:, k:k + 1],
                        float(128 * c), op0=ALU.add, op1=ALU.add))
                EMIT("v", lambda e, k=k: e.tensor_scalar(
                    wixf[:, 4:5], iota_p[:, :], bcast_t[:, 48 + k:49 + k],
                    None, op0=ALU.add))
                DR("v")
                EMIT("v", lambda e, kp=kp: e.tensor_copy(wix[:, kp * 5:kp * 5 + 5], wixf[:, 0:5]))
                DR("v")
                EMIT("v", lambda e, k=k: e.tensor_scalar(
                    ixf[0:32, 0:1], iota_p[0:32, :], bcast_t[0:32, 32 + k:33 + k],
                    None, op0=ALU.add))
                DR("v")
                EMIT("v", lambda e, kp=kp: e.tensor_copy(b1ix[:, kp:kp + 1],
                                                         ixf[0:32, 0:1]))
                DR("v")
                EMIT("v", lambda e, k=k: e.tensor_scalar_mul(
                    ixf[0:2, 1:2], bcast_t[0:2, k:k + 1], 1.0 / 512.0))
                DR("v")
                vidx = EMIT("v", lambda e: e.tensor_copy(b2ix[:, 0:1], ixf[0:2, 1:2]), "tV")
                DR("v")
                W("g", "tV", vidx)
                ls_n[kp] += 6
                for c in range(4):
                    EMIT("g", lambda e, kp=kp, c=c: e.indirect_dma_start(
                        out=ls8[kp * 4 + c][:, :],
                        out_offset=None, in_=lst_h,
                        in_offset=bass.IndirectOffsetOnAxis(ap=lsix[:, kp * 4 + c:kp * 4 + c + 1], axis=0)),
                        dls, 16)
                if b1st_free[kp] is not None:
                    W("g", b1st_free[kp][0], b1st_free[kp][1])
                EMIT("g", lambda e, kp=kp: e.indirect_dma_start(
                    out=b1st2[kp][:, :], out_offset=None,
                    in_=b1_h.rearrange("e (a b) -> (e a) b", b=128),
                    in_offset=bass.IndirectOffsetOnAxis(ap=b1ix[:, kp:kp + 1], axis=0)),
                    dls, 16)
                if b2f_free[0] is not None:
                    W("g", b2f_free[0][0], b2f_free[0][1])
                EMIT("g", lambda e: e.indirect_dma_start(
                    out=b2f[:, :], out_offset=None,
                    in_=b2_h,
                    in_offset=bass.IndirectOffsetOnAxis(ap=b2ix[:, 0:1], axis=0)),
                    dls, 16)
                W("v", dls, 16 * ls_n[kp])
                if g_n[kp] > 0:
                    W("v", dg, 16 * g_n[kp])
                W("v", "dY0", 16 * y_n["dY0"])
                W("v", "dY1", 16 * y_n["dY1"])
                DR("v")
                for c in range(4):
                    EMIT("v", lambda e, kp=kp, c=c: e.tensor_copy(
                        tok_i[:, kp * 4 + c:kp * 4 + c + 1], ls8[kp * 4 + c][:, 0:1]))
                DR("v")
                vb2c = EMIT("v", lambda e, kp=kp: e.tensor_copy(
                    b2r[:, kp * D:(kp + 1) * D], b2f[0:1, 0:D]), "tV")
                b2f_free[0] = ("tV", vb2c)
                DR("v")
                for c in range(3):
                    EMIT("v", lambda e, kp=kp, c=c: e.tensor_copy(
                        dst_i[:, kp * 4 + c:kp * 4 + c + 1], ls8[kp * 4 + c][:, 1:2]))
                DR("v")
                vdst = EMIT("v", lambda e, kp=kp: e.tensor_copy(
                    dst_i[:, kp * 4 + 3:kp * 4 + 4], ls8[kp * 4 + 3][:, 1:2]), "tV")
                DR("v")
                ls_free[kp] = ("tV", vdst)
                W("g", "tV", vdst)
                for c in range(4):
                    if xg_free[c] is not None:
                        W("g", xg_free[c][0], xg_free[c][1])
                    g_n[kp] += 1
                    EMIT("g", lambda e, kp=kp, c=c: e.indirect_dma_start(
                        out=xg4[c][:, :], out_offset=None,
                        in_=xnd_h,
                        in_offset=bass.IndirectOffsetOnAxis(ap=tok_i[:, kp * 4 + c:kp * 4 + c + 1], axis=0)),
                        dg, 16)
                w1q_info = [None] * 4
                w1q_infos[kp] = w1q_info

                def load_w1q(q, kp=kp):
                    wp = w1ctr_l[0] % 2
                    w1ctr_l[0] += 1
                    if w1_free[wp] is not None:
                        W("g", w1_free[wp][0], w1_free[wp][1])
                    w1_n[wp] += 1
                    EMIT("g", lambda e, kp=kp, q=q, wp=wp: e.indirect_dma_start(
                        out=w1r2[wp][:, :], out_offset=None,
                        in_=w1bf_h,
                        in_offset=bass.IndirectOffsetOnAxis(
                            ap=wix[:, kp * 5 + q:kp * 5 + q + 1], axis=0)),
                        "dW1%s" % ("a" if wp == 0 else "b"), 16)
                    w1q_info[q] = (wp, 16 * w1_n[wp])

                load_w1q(0)
                load_w1q(1)
                load_w1q_l[kp] = load_w1q


            def load_w2(k):
                kp = k % 2
                if w2_free[0] is not None:
                    W("g", w2_free[0][0], w2_free[0][1])
                w2_n[0] += 1
                EMIT("g", lambda e, kp=kp: e.indirect_dma_start(
                    out=w2full[:, :], out_offset=None,
                    in_=w2bf_h,
                    in_offset=bass.IndirectOffsetOnAxis(
                        ap=wix[:, kp * 5 + 4:kp * 5 + 5], axis=0)),
                    "dW2a", 16)

            for k in range(TMAX):
                kp = k % 2
                dls = "dLs%d" % kp
                dg = "dG%d" % kp
                if k == 0:
                    front(0)
                    load_w2(0)
                w1q_info = w1q_infos[kp]
                bank_wait("t", "pH0")
                W("t", dls, 16 * ls_n[kp])
                vtr = EMIT("t", lambda e, kp=kp: e.transpose(
                    pH[0][:, 0:32], b1st2[kp][:, :], id_f[0:32, 0:32]), "tT")
                b1st_free[kp] = ("tT", vtr)
                bank_done("t", "pH0", "tT", vtr)
                bank_wait("v", "pH0")
                if b1sb_free[kp] is not None:
                    W("v", b1sb_free[kp][0], b1sb_free[kp][1])
                DR("v")
                vb1e = EMIT("v", lambda e, kp=kp: e.tensor_copy(
                    b1sb[:, kp * 32:(kp + 1) * 32], pH[0][:, 0:32]), "tV")
                DR("v")
                bank_done("v", "pH0", "tV", vb1e)
                b1_ready = vb1e
                last_xgT = None
                for c in range(4):
                    bank_wait("t", "pbT2")
                    W("t", dg, 16 * g_n[kp])
                    if c == 0 and xgT_free[kp] is not None:
                        W("t", xgT_free[kp][0], xgT_free[kp][1])
                    def _trc(e, kp=kp, c=c):
                        ins = None
                        for d in range(ND):
                            ins = e.transpose(pbT2[:, d * 128:(d + 1) * 128],
                                              xg4[c][:, d * 128:(d + 1) * 128],
                                              id_b[:, :])
                        return ins
                    vtc = EMIT("t", _trc, "tT")
                    bank_done("t", "pbT2", "tT", vtc)
                    xg_free[c] = ("tT", vtc)
                    bank_wait("v", "pbT2")
                    DR("v")
                    def _evc(e, kp=kp, c=c):
                        ins = None
                        for d in range(ND):
                            ins = e.tensor_copy(
                                xgT[:, kp * ND * 512 + d * 512 + c * 128:
                                    kp * ND * 512 + d * 512 + (c + 1) * 128],
                                pbT2[:, d * 128:(d + 1) * 128])
                        return ins
                    vev = EMIT("v", _evc, "tV")
                    DR("v")
                    bank_done("v", "pbT2", "tV", vev)
                    last_xgT = vev

                hT_evacs = []
                for q in range(4):
                    wp, wv_ = w1q_info[q]
                    W("t", "dW1%s" % ("a" if wp == 0 else "b"), wv_)
                    vw1m = None
                    for ft in range(8):
                        bank = 1 + (q * 8 + ft) % 3
                        bank_wait("t", "pH%d" % bank)
                        if q == 0 and ft == 0:
                            W("t", "tV", last_xgT)
                        def _w1(e, kp=kp, wp=wp, ft=ft, bank=bank):
                            ins = None
                            for d in range(ND):
                                ins = e.matmul(
                                    pH[bank][:, :],
                                    w1r2[wp][:, d * 1024 + ft * 128: d * 1024 + (ft + 1) * 128],
                                    xgT[:, kp * ND * 512 + d * 512: kp * ND * 512 + (d + 1) * 512],
                                    start=(d == 0), stop=(d == ND - 1))
                            return ins
                        vw1m = EMIT("t", _w1, "tT")
                        bank_done("t", "pH%d" % bank, "tT", vw1m)
                        ekey = "s" if ft % 2 == 0 else "v"
                        bank_wait(ekey, "pH%d" % bank)
                        if ekey == "s":
                            W("s", "tV", b1_ready)
                            vev = EMIT("s", lambda e, q=q, ft=ft, kp=kp, bank=bank: e.activation(
                                hT[:, (q * 8 + ft) * 512:(q * 8 + ft + 1) * 512],
                                pH[bank][:, :], AF.Relu,
                                bias=b1sb[:, kp * 32 + 8 * q + ft: kp * 32 + 8 * q + ft + 1]), "tS")
                        else:
                            vev = EMIT("v", lambda e, q=q, ft=ft, kp=kp, bank=bank: e.tensor_scalar(
                                hT[:, (q * 8 + ft) * 512:(q * 8 + ft + 1) * 512],
                                pH[bank][:, :],
                                b1sb[:, kp * 32 + 8 * q + ft: kp * 32 + 8 * q + ft + 1], 0.0,
                                op0=ALU.add, op1=ALU.max), "tV")
                        bank_done(ekey, "pH%d" % bank, OWN[ekey], vev)
                        hT_evacs.append((ekey, vev))
                    w1_free[wp] = ("tT", vw1m)
                    if q == 3:
                        sev = [v for (ek2, v) in hT_evacs if ek2 == "s"]
                        if sev:
                            b1sb_free[kp] = ("tS", sev[-1])
                    if q + 2 < 4:
                        load_w1q_l[kp](q + 2)

                if k + 1 < TMAX:
                    front(k + 1)
                for sh in range(2):
                    for b in range(4):
                        bank_wait("t", "pH%d" % b)
                    if sh == 0:
                        for (ekey, vev) in hT_evacs:
                            W("t", OWN[ekey], vev)
                    if sh == 0:
                        W("t", "dW2a", 16 * w2_n[0])
                    def _y(e, kp=kp, sh=sh):
                        ins = None
                        for gf in range(32):
                            for ci in range(2):
                                c = sh * 2 + ci
                                for dh in range(2):
                                    ins = e.matmul(
                                        pH[ci * 2 + dh][:, :],
                                        hT[:, gf * 512 + 128 * c: gf * 512 + 128 * (c + 1)],
                                        w2full[:, gf * D + 512 * dh: gf * D + 512 * (dh + 1)],
                                        start=(gf == 0), stop=False)
                        return ins
                    vym = EMIT("t", _y, "tT")
                    if sh == 1:
                        w2_free[0] = ("tT", vym)
                    for b in range(4):
                        bank_done("t", "pH%d" % b, "tT", vym)
                    def _b2(e, kp=kp):
                        ins = None
                        for ci in range(2):
                            for dh in range(2):
                                ins = e.matmul(
                                    pH[ci * 2 + dh][:, :],
                                    ones_b[:, 0:128],
                                    b2r[:, kp * D + 512 * dh: kp * D + 512 * (dh + 1)],
                                    start=False, stop=True)
                        return ins
                    vym = EMIT("t", _b2, "tT")
                    for b in range(4):
                        bank_done("t", "pH%d" % b, "tT", vym)
                    for ci in range(2):
                        c = sh * 2 + ci
                        yp = c % 2
                        dyn = "dY%d" % yp
                        for b in (ci * 2, ci * 2 + 1):
                            bank_wait("v", "pH%d" % b)
                        if ysb_free[yp] is not None:
                            W("v", dyn, 16 * y_n[dyn])
                        DR("v")
                        def _ge(e, kp=kp, c=c, ci=ci, yp=yp):
                            ins = None
                            for dh in range(2):
                                ins = e.tensor_scalar(
                                    ysb2[yp][:, 512 * dh: 512 * (dh + 1)],
                                    pH[ci * 2 + dh][:, :],
                                    ls8[kp * 4 + c][:, 2:3],
                                    None, op0=ALU.mult)
                            return ins
                        vge = EMIT("v", _ge, "tV")
                        DR("v")
                        for b in (ci * 2, ci * 2 + 1):
                            bank_done("v", "pH%d" % b, "tV", vge)
                        ysb_free[yp] = ("tV", vge)
                        ls_free[kp] = ("tV", vge)
                        W("g", "tV", vge)
                        W("g", "dY0", 16 * y_n["dY0"])
                        W("g", "dY1", 16 * y_n["dY1"])
                        if not xadds_waited[0]:
                            W("g", "dPoA", P.sem_val["dPoA"])
                            xadds_waited[0] = True
                        y_n[dyn] += 1
                        EMIT("g", lambda e, kp=kp, c=c, yp=yp: e.indirect_dma_start(
                            out=out_h,
                            out_offset=bass.IndirectOffsetOnAxis(
                                ap=dst_i[:, kp * 4 + c: kp * 4 + c + 1], axis=0),
                            in_=ysb2[yp][:, :], in_offset=None,
                            bounds_check=e._bnd_yslt, oob_is_err=False,
                            compute_op=ALU.add), dyn, 16)
                if k + 1 < TMAX:
                    load_w2(k + 1)

            for ek in ["t", "s", "v"]:
                W(ek, OWN[ek], P.sem_val[OWN[ek]])
                P.add(ek, lambda e: e.nop().then_inc(sem["ph"], 1))
                P.inc("ph")
            W("y", "dA", P.sem_val["dA"])
            P.add("y", lambda e: e.nop().then_inc(sem["ph"], 1))
            P.inc("ph")
            for sn in ["dW1a", "dW1b", "dW2a", "dW2b", "dLs0", "dLs1",
                       "dG0", "dG1", "dY0", "dY1", "dOut", "dPoA", "dPoB"]:
                W("g", sn, P.sem_val.get(sn, 0))
            P.add("g", lambda e: e.nop().then_inc(sem["ph"], 1))
            bar_f = P.inc("ph")
        for ek in ["t", "s", "v", "g", "y"]:
            W(ek, "ph", bar_f)

        @block.tensor
        def _(eng):
            for fn in P.steps["t"]:
                fn(eng)

        @block.scalar
        def _(eng):
            for fn in P.steps["s"]:
                fn(eng)

        @block.vector
        def _(eng):
            for fn in P.steps["v"]:
                fn(eng)

        @block.gpsimd
        def _(eng):
            for fn in P.steps["g"]:
                fn(eng)

        @block.sync
        def _(eng):
            for fn in P.steps["y"]:
                fn(eng)

    return nc


def kernel(L0_attn_x, L0_attn_wq, L0_attn_wk, L0_attn_wv, L0_attn_wo,
           L0_ffn_Wr, L0_ffn_br, L0_ffn_W1, L0_ffn_b1, L0_ffn_W2, L0_ffn_b2):
    x = np.ascontiguousarray(np.asarray(L0_attn_x, dtype=np.float32))
    B = x.shape[0]
    shared = dict(
        wq=np.ascontiguousarray(np.asarray(L0_attn_wq, np.float32)),
        wk=np.ascontiguousarray(np.asarray(L0_attn_wk, np.float32)),
        wv=np.ascontiguousarray(np.asarray(L0_attn_wv, np.float32)),
        wo=np.ascontiguousarray(np.asarray(L0_attn_wo, np.float32)),
        wr=np.ascontiguousarray(np.asarray(L0_ffn_Wr, np.float32)),
        br=np.ascontiguousarray(np.asarray(L0_ffn_br, np.float32)).reshape(1, E),
        w1=np.ascontiguousarray(np.asarray(L0_ffn_W1, np.float32)),
        b1=np.ascontiguousarray(np.asarray(L0_ffn_b1, np.float32)),
        w2=np.ascontiguousarray(np.asarray(L0_ffn_W2, np.float32)),
        b2=np.ascontiguousarray(np.asarray(L0_ffn_b2, np.float32)),
        ident=np.eye(128, dtype=np.float32),
        tril=np.triu(np.ones((128, 128), np.float32)),
    )
    nc = build_nc()
    in_maps = [dict(x=x[c], **shared) for c in range(B)]
    res = run_bass_kernel_spmd(nc, in_maps, core_ids=list(range(B)))
    out = np.stack([res.results[c]["out"] for c in range(B)], axis=0)
    return out


# version 14
# speedup vs baseline: 1.1022x; 1.1022x over previous
import contextlib
import numpy as np
import concourse.bass as bass
import concourse.mybir as mybir
from concourse.bass_utils import run_bass_kernel_spmd
from bass_rust import RuntimeValue

F32 = mybir.dt.float32
I32 = mybir.dt.int32
BF16 = mybir.dt.bfloat16
AF = mybir.ActivationFunctionType
ALU = mybir.AluOpType
AX = mybir.AxisListType

S = 2048
D = 1024
NH = 16
E = 8
NT = S // 128
ND = D // 128
NQC = S // 512
TMAX = 15
NCH = 4 * E + 1
HUGE = float(1 << 16)


class Plan:
    def __init__(self):
        self.steps = {"t": [], "s": [], "v": [], "g": [], "y": []}
        self.sem_val = {}

    def add(self, eng, fn):
        self.steps[eng].append(fn)

    def inc(self, sem_name, amount=1):
        v = self.sem_val.get(sem_name, 0) + amount
        self.sem_val[sem_name] = v
        return v


def build_nc():
    nc = bass.Bass("TRN2", target_bir_lowering=False)

    x_h = nc.dram_tensor("x", [S, D], F32, kind="ExternalInput").ap()
    wq_h = nc.dram_tensor("wq", [D, D], F32, kind="ExternalInput").ap()
    wk_h = nc.dram_tensor("wk", [D, D], F32, kind="ExternalInput").ap()
    wv_h = nc.dram_tensor("wv", [D, D], F32, kind="ExternalInput").ap()
    wo_h = nc.dram_tensor("wo", [D, D], F32, kind="ExternalInput").ap()
    wr_h = nc.dram_tensor("wr", [D, E], F32, kind="ExternalInput").ap()
    br_h = nc.dram_tensor("br", [1, E], F32, kind="ExternalInput").ap()
    w1_h = nc.dram_tensor("w1", [E, D, 4096], F32, kind="ExternalInput").ap()
    b1_h = nc.dram_tensor("b1", [E, 4096], F32, kind="ExternalInput").ap()
    w2_h = nc.dram_tensor("w2", [E, 4096, D], F32, kind="ExternalInput").ap()
    b2_h = nc.dram_tensor("b2", [E, D], F32, kind="ExternalInput").ap()
    id_h = nc.dram_tensor("ident", [128, 128], F32, kind="ExternalInput").ap()
    tril_h = nc.dram_tensor("tril", [128, 128], F32, kind="ExternalInput").ap()
    out_h = nc.dram_tensor("out", [S, D], F32, kind="ExternalOutput").ap()
    xnd_h = nc.dram_tensor("xnd", [S, D], BF16, kind="Internal").ap()
    w1bf_h = nc.dram_tensor("w1bf", [E * 4 * 128, 8 * 1024], BF16, kind="Internal").ap()
    w2bf_h = nc.dram_tensor("w2bf", [E * 128, 32 * 1024], BF16, kind="Internal").ap()
    lst_h = nc.dram_tensor("lst", [NCH * 512 + 1, 8], F32, kind="Internal").ap()

    P = Plan()
    SEMS = ["dX0", "dX1", "dA", "dC", "dWq", "dWo", "dW1a", "dW1b",
            "dW2a", "dW2b", "dLs0", "dLs1", "dG0", "dG1", "dSc",
            "dPa", "dPb", "dPoA", "dPoB", "dXd0", "dXd1", "dY0", "dY1", "dOut",
            "tT", "tS", "tV", "ph"]
    OWN = {"t": "tT", "s": "tS", "v": "tV"}

    with contextlib.ExitStack() as ctx:
        sem = {n: ctx.enter_context(nc.semaphore(n)) for n in SEMS}

        def W(eng_key, sname, val):
            if val > 0:
                P.add(eng_key, lambda e, s=sem[sname], v=val: e.wait_ge(s, v))

        def DR(eng_key):
            P.add(eng_key, lambda e: e.drain())

        def EMIT(eng_key, fn, inc_sem=None, inc_by=1):
            if inc_sem is None:
                P.add(eng_key, fn)
                return None
            v = P.inc(inc_sem, inc_by)
            P.add(eng_key, lambda e, f=fn, s=sem[inc_sem], a=inc_by: f(e).then_inc(s, a))
            return v

        bank_last = {}

        def bank_wait(eng_key, bank):
            la = bank_last.get(bank)
            if la is not None and la[0] != eng_key:
                W(eng_key, la[1], la[2])

        def bank_done(eng_key, bank, sname, val):
            bank_last[bank] = (eng_key, sname, val)

        gates = ctx.enter_context(nc.sbuf_tensor("gates", [128, NT * E], F32))
        id_f = ctx.enter_context(nc.sbuf_tensor("id_f", [128, 128], F32))
        id_b = ctx.enter_context(nc.sbuf_tensor("id_b", [128, 128], BF16))
        tril_b = ctx.enter_context(nc.sbuf_tensor("tril_b", [128, 128], BF16))
        br_t = ctx.enter_context(nc.sbuf_tensor("br_t", [128, E], F32))
        ones_b = ctx.enter_context(nc.sbuf_tensor("ones_b", [1, 128], BF16))
        eps_t = ctx.enter_context(nc.sbuf_tensor("eps_t", [128, 1], F32))
        lg_all = ctx.enter_context(nc.sbuf_tensor("lg_all", [128, NT * E], F32))
        ind_all = ctx.enter_context(nc.sbuf_tensor("ind_all", [128, NT * E], F32))
        ind2_all = ctx.enter_context(nc.sbuf_tensor("ind2_all", [128, NT * E], F32))
        iota_p = ctx.enter_context(nc.sbuf_tensor("iota_p", [128, 1], F32))
        iota_eb = ctx.enter_context(nc.sbuf_tensor("iota_eb", [128, E], F32))
        iok_b = ctx.enter_context(nc.sbuf_tensor("iok_b", [E, 16], F32))
        ones8 = ctx.enter_context(nc.sbuf_tensor("ones8", [8, 1], BF16))
        exsched = ctx.enter_context(nc.sbuf_tensor("exsched", [1, 16], I32))
        bcast_t = ctx.enter_context(nc.sbuf_tensor("bcast_t", [128, 64], F32))

        rt8 = ctx.enter_context(nc.sbuf_tensor("rt8", [128, 6 * E], F32))
        rs1 = ctx.enter_context(nc.sbuf_tensor("rs1", [128, 8], F32))
        ixj = ctx.enter_context(nc.sbuf_tensor("ixj", [128, NT], I32))
        wr_sb = ctx.enter_context(nc.sbuf_tensor("wr_sb", [128, ND * E], F32))
        qkv_scope = ctx.enter_context(contextlib.ExitStack())
        qT = qkv_scope.enter_context(nc.sbuf_tensor("qT", [128, ND * S], BF16))
        kT = qkv_scope.enter_context(nc.sbuf_tensor("kT", [128, ND * S], BF16))
        v_sb = qkv_scope.enter_context(nc.sbuf_tensor("v_sb", [128, NT * NH * 65], BF16))
        acc_scope = ctx.enter_context(contextlib.ExitStack())
        acc = None

        block = ctx.enter_context(nc.Block())

        EMIT("y", lambda e: e.dma_start(out=id_f[:, :], in_=id_h), "dA", 16)
        EMIT("g", lambda e: e.dma_start(out=id_b[:, :], in_=id_h), "dC", 16)
        EMIT("g", lambda e: e.dma_start(out=tril_b[:, :], in_=tril_h), "dC", 16)
        EMIT("g", lambda e: e.dma_start(out=br_t[:, :], in_=br_h.partition_broadcast(128)), "dC", 16)
        EMIT("g", lambda e: e.dma_start(
            out=wr_sb[:, :].rearrange("p (t e) -> p t e", t=ND),
            in_=wr_h.rearrange("(t p) e -> p t e", p=128)), "dC", 16)
        vC = P.sem_val["dC"]
        EMIT("v", lambda e: e.memset(ones_b[:, :], 1.0), "tV")
        EMIT("v", lambda e: e.memset(ones8[:, :], 1.0), "tV")
        def _setup_bounds(e):
            r1 = e.alloc_register("bnd_lst")
            e.reg_mov(r1, NCH * 512 - 1)
            e._bnd_lst = RuntimeValue(val=r1)
            r2 = e.alloc_register("bnd_yslt")
            e.reg_mov(r2, S - 1)
            e._bnd_yslt = RuntimeValue(val=r2)
        P.add("g", _setup_bounds)
        EMIT("g", lambda e: e.iota(iota_p[:, :], pattern=[[1, 1]], base=0,
                                   channel_multiplier=1,
                                   allow_small_or_imprecise_dtypes=True))
        EMIT("g", lambda e: e.iota(iota_eb[:, :], pattern=[[1, E]], base=0,
                                   channel_multiplier=0,
                                   allow_small_or_imprecise_dtypes=True))
        EMIT("g", lambda e: e.iota(iok_b[:, :], pattern=[[1, 16]], base=0,
                                   channel_multiplier=0,
                                   allow_small_or_imprecise_dtypes=True))
        EMIT("g", lambda e: e.iota(ixj[:, :], pattern=[[128, NT]], base=0,
                                   channel_multiplier=1,
                                   allow_small_or_imprecise_dtypes=True))
        veps = EMIT("v", lambda e: e.memset(eps_t[:, :], 1e-5), "tV")
        EMIT("v", lambda e: e.memset(
            v_sb[:, :].rearrange("p (g c) -> p g c", c=65)[:, :, 64:65], 1.0), "tV")

        pre_ctr = [0]

        def emit_prestage(n):
            for _ in range(n):
                mm = pre_ctr[0]
                if mm >= 64:
                    return
                pre_ctr[0] += 1
                is_w1 = mm < 32
                qid = mm if is_w1 else mm - 32
                e_i2, q2 = qid // 4, qid % 4
                if is_w1:
                    EMIT("g", lambda e, e_i2=e_i2, q2=q2, qid=qid: e.dma_start(
                        out=w1bf_h.rearrange("(qq p) f2 -> qq p f2", p=128)[qid]
                        .rearrange("p (t ff) -> p t ff", ff=1024),
                        in_=w1_h[e_i2][:, 1024 * q2:1024 * (q2 + 1)].rearrange(
                            "(t p) ff -> p t ff", p=128)),
                        "dPa", 16)
                else:
                    EMIT("g", lambda e, e_i2=e_i2, q2=q2: e.dma_start(
                        out=w2bf_h.rearrange("(ee p) f2 -> ee p f2", p=128)[e_i2]
                        [:, q2 * 8192:(q2 + 1) * 8192].rearrange("p (t dd) -> p t dd", dd=1024),
                        in_=w2_h[e_i2].rearrange("(q t p) d -> q p t d", q=4, p=128)[q2]),
                        "dPa", 16)

        with contextlib.ExitStack() as a1:
            xnT = a1.enter_context(nc.sbuf_tensor("xnT", [128, ND * S], BF16))
            x_sb = a1.enter_context(nc.sbuf_tensor("x_sb", [128, 2 * D], F32))
            xnb = a1.enter_context(nc.sbuf_tensor("xnb", [128, 2 * D], BF16))
            bs_t = a1.enter_context(nc.sbuf_tensor("bs_t", [128, 2 * 12], F32))
            ba_t = a1.enter_context(nc.sbuf_tensor("ba_t", [128, 2 * 2], F32))
            st1 = a1.enter_context(nc.sbuf_tensor("st1", [128, 8], F32))
            w_sb = a1.enter_context(nc.sbuf_tensor("w_sb", [128, 3 * ND * D], BF16))

            xnTf = a1.enter_context(nc.sbuf_tensor("xnTf", [128, 1024], F32))
            pbR = a1.enter_context(nc.psum_tensor("pbR", [128, 1024], F32))
            pb0 = a1.enter_context(nc.psum_tensor("pb0", [128, 512], F32))
            pb1 = a1.enter_context(nc.psum_tensor("pb1", [128, 512], F32))
            pb2 = a1.enter_context(nc.psum_tensor("pb2", [128, 512], F32))

            for wi, wh in enumerate([wq_h, wk_h, wv_h]):
                EMIT("g", lambda e, wi=wi, wh=wh: e.dma_start(
                    out=w_sb[:, wi * ND * D:(wi + 1) * ND * D].rearrange("p (t f) -> p t f", t=ND),
                    in_=wh.rearrange("(t p) f -> p t f", p=128)), "dWq", 16)
            vwqkv = P.sem_val["dWq"]

            x_slot_free = [None, None]
            xnd_dumped = [None, None]
            last_xn_evac = None
            xnTf_free = [None]
            qkv_par = [0]

            def emit_qkv_chunk(sc, xn_ready):
                for wi, dst in ((0, qT), (1, kT)):
                    for u in range(ND):
                        par = qkv_par[0] % 2
                        qkv_par[0] += 1
                        pb = pb0 if par == 0 else pb1
                        pbn = "pb0" if par == 0 else "pb1"
                        W("t", "dWq", vwqkv)
                        W("t", "tV", xn_ready)
                        bank_wait("t", pbn)
                        def _mm(e, wi=wi, u=u, sc=sc, pb=pb):
                            ins = None
                            for d in range(ND):
                                ins = e.matmul(pb[:, :],
                                               w_sb[:, wi * ND * D + d * D + 128 * u:
                                                    wi * ND * D + d * D + 128 * (u + 1)],
                                               xnT[:, d * S + 512 * sc: d * S + 512 * (sc + 1)],
                                               start=(d == 0), stop=(d == ND - 1))
                            return ins
                        vmm = EMIT("t", _mm, "tT")
                        bank_done("t", pbn, "tT", vmm)
                        ekey = "s" if par == 0 else "v"
                        bank_wait(ekey, pbn)
                        if ekey == "s":
                            vev = EMIT("s", lambda e, u=u, sc=sc, pb=pb, dst=dst: e.activation(
                                dst[:, u * S + 512 * sc: u * S + 512 * (sc + 1)],
                                pb[:, :], AF.Copy), "tS")
                        else:
                            vev = EMIT("v", lambda e, u=u, sc=sc, pb=pb, dst=dst: e.tensor_copy(
                                dst[:, u * S + 512 * sc: u * S + 512 * (sc + 1)],
                                pb[:, :]), "tV")
                        bank_done(ekey, pbn, OWN[ekey], vev)
                for j2 in range(4 * sc, 4 * sc + 4):
                    for ch in range(2):
                        par = qkv_par[0] % 2
                        qkv_par[0] += 1
                        pb = pb0 if par == 0 else pb1
                        pbn = "pb0" if par == 0 else "pb1"
                        W("t", "dWq", vwqkv)
                        W("t", "tV", xn_ready)
                        bank_wait("t", pbn)
                        def _mmv(e, j2=j2, ch=ch, pb=pb):
                            ins = None
                            for d in range(ND):
                                ins = e.matmul(pb[:, :],
                                               xnT[:, d * S + 128 * j2: d * S + 128 * (j2 + 1)],
                                               w_sb[:, 2 * ND * D + d * D + 512 * ch:
                                                    2 * ND * D + d * D + 512 * (ch + 1)],
                                               start=(d == 0), stop=(d == ND - 1))
                            return ins
                        vmm = EMIT("t", _mmv, "tT")
                        bank_done("t", pbn, "tT", vmm)
                        ekey = "s" if ch == 0 else "v"
                        bank_wait(ekey, pbn)
                        base = j2 * NH * 65 + ch * 8 * 65
                        if ekey == "s":
                            vev = EMIT("s", lambda e, base=base, pb=pb: e.activation(
                                v_sb[:, base:base + 8 * 65].rearrange("p (h c) -> p h c", c=65)[:, :, 0:64],
                                pb[:, :].rearrange("p (h c) -> p h c", c=64), AF.Copy), "tS")
                        else:
                            vev = EMIT("v", lambda e, base=base, pb=pb: e.tensor_copy(
                                v_sb[:, base:base + 8 * 65].rearrange("p (h c) -> p h c", c=65)[:, :, 0:64],
                                pb[:, :].rearrange("p (h c) -> p h c", c=64)), "tV")
                        bank_done(ekey, pbn, OWN[ekey], vev)

            for j in range(NT):
                slot = j % 2
                dxs = "dX0" if slot == 0 else "dX1"
                if x_slot_free[slot] is not None:
                    for sname, val in x_slot_free[slot]:
                        W("y", sname, val)
                vload = EMIT("y", lambda e, j=j, slot=slot: e.dma_start(
                    out=x_sb[:, slot * D:(slot + 1) * D],
                    in_=x_h.rearrange("(j p) d -> j p d", p=128)[j]), dxs, 16)
                W("v", dxs, vload)
                so = slot * 12
                sa = slot * 2
                EMIT("v", lambda e, slot=slot, so=so: e.bn_stats(
                    bs_t[:, so:so + 6], x_sb[:, slot * D: slot * D + 512]))
                EMIT("v", lambda e, slot=slot, so=so: e.bn_stats(
                    bs_t[:, so + 6:so + 12], x_sb[:, slot * D + 512:(slot + 1) * D]))
                DR("v")
                vvar = EMIT("v", lambda e, so=so, sa=sa: e.bn_aggr(
                    ba_t[:, sa:sa + 2], bs_t[:, so:so + 12]), "tV")
                W("s", "tV", vvar)
                EMIT("s", lambda e, sa=sa, slot=slot: e.activation(
                    st1[:, 2 + slot:3 + slot], ba_t[:, sa + 1:sa + 2], AF.Ln,
                    bias=eps_t[:, 0:1], scale=1.0))
                DR("s")
                vrstd = EMIT("s", lambda e, slot=slot: e.activation(
                    st1[:, 4 + slot:5 + slot], st1[:, 2 + slot:3 + slot], AF.Exp,
                    scale=-0.5), "tS")
                W("v", "tS", vrstd)
                DR("v")
                vxn = EMIT("v", lambda e, slot=slot, sa=sa: e.tensor_scalar(
                    x_sb[:, slot * D:(slot + 1) * D], x_sb[:, slot * D:(slot + 1) * D],
                    ba_t[:, sa:sa + 1], st1[:, 4 + slot:5 + slot],
                    op0=ALU.subtract, op1=ALU.mult), "tV")
                W("s", "tV", vxn)
                if xnd_dumped[slot] is not None:
                    W("s", "dXd%d" % slot, xnd_dumped[slot])
                vxnb = EMIT("s", lambda e, slot=slot: e.activation(
                    xnb[:, slot * D:(slot + 1) * D],
                    x_sb[:, slot * D:(slot + 1) * D], AF.Copy), "tS")
                W("y", "tS", vxnb)
                xnd_dumped[slot] = EMIT("y", lambda e, j=j, slot=slot: e.dma_start(
                    out=xnd_h.rearrange("(j p) d -> j p d", p=128)[j],
                    in_=xnb[:, slot * D:(slot + 1) * D]), "dXd%d" % slot, 16)
                W("t", "tV", vxn)
                W("t", "dC", vC)
                if j == 0:
                    W("t", "dA", 16)
                bank_wait("t", "pbR")
                def _tr(e, j=j, slot=slot):
                    ins = None
                    for d in range(ND):
                        ins = e.transpose(pbR[:, d * 128:(d + 1) * 128],
                                          x_sb[:, slot * D + d * 128: slot * D + (d + 1) * 128],
                                          id_f[:, :])
                    return ins
                vtr = EMIT("t", _tr, "tT")
                x_slot_free[slot] = [("tS", vxnb), ("tT", vtr)]
                bank_done("t", "pbR", "tT", vtr)
                emit_prestage(2)
                bank_wait("v", "pbR")
                def _ev(e, j=j):
                    ins = None
                    for d in range(ND):
                        ins = e.tensor_copy(xnT[:, d * S + 128 * j: d * S + 128 * (j + 1)],
                                            pbR[:, d * 128:(d + 1) * 128])
                    return ins
                vev = EMIT("v", _ev, "tV")
                last_xn_evac = vev
                if xnTf_free[0] is not None:
                    W("v", xnTf_free[0][0], xnTf_free[0][1])
                DR("v")
                vef = EMIT("v", lambda e: e.tensor_copy(xnTf[:, :], pbR[:, 0:1024]), "tV")
                bank_done("v", "pbR", "tV", vef)

                W("t", "tV", vef)
                bank_wait("t", "pb2")
                def _rt(e):
                    ins = None
                    for d in range(ND):
                        ins = e.matmul(pb2[:, 0:E],
                                       xnTf[:, d * 128:(d + 1) * 128],
                                       wr_sb[:, d * E:(d + 1) * E],
                                       start=(d == 0), stop=(d == ND - 1))
                    return ins
                vrt = EMIT("t", _rt, "tT")
                xnTf_free[0] = ("tT", vrt)
                bank_done("t", "pb2", "tT", vrt)
                bank_wait("v", "pb2")
                vpb2 = EMIT("v", lambda e, j=j: e.tensor_add(
                    lg_all[:, j * E:(j + 1) * E], pb2[:, 0:E], br_t[:, :]), "tV")
                bank_done("v", "pb2", "tV", vpb2)
                DR("v")
                if j % 4 == 3:
                    emit_qkv_chunk(j // 4, last_xn_evac)

            for ek in ["t", "s", "v"]:
                W(ek, OWN[ek], P.sem_val[OWN[ek]])
                P.add(ek, lambda e: e.nop().then_inc(sem["ph"], 1))
                P.inc("ph")
            W("y", "dA", P.sem_val["dA"])
            W("y", "dX0", P.sem_val["dX0"])
            W("y", "dX1", P.sem_val["dX1"])
            W("y", "dXd0", P.sem_val["dXd0"])
            W("y", "dXd1", P.sem_val["dXd1"])
            P.add("y", lambda e: e.nop().then_inc(sem["ph"], 1))
            P.inc("ph")
            W("g", "dWq", P.sem_val["dWq"])
            W("g", "dC", P.sem_val["dC"])
            P.add("g", lambda e: e.nop().then_inc(sem["ph"], 1))
            bar_a1 = P.inc("ph")
        for ek in ["t", "s", "v", "g", "y"]:
            W(ek, "ph", bar_a1)
        bank_last.clear()
        acc = acc_scope.enter_context(nc.sbuf_tensor("acc", [128, NT * D], F32, side="right"))

        with contextlib.ExitStack() as a2:
            pT = a2.enter_context(nc.sbuf_tensor("pT", [128, 16 * 512], BF16))
            ctx_all = a2.enter_context(nc.sbuf_tensor("ctx_all", [128, 8 * 512], BF16))
            wo_sb = a2.enter_context(nc.sbuf_tensor("wo_sb", [128, 8 * 1024], BF16))
            rrow_f = a2.enter_context(nc.sbuf_tensor("rrow_f", [1, 512], BF16))
            rrow_b = a2.enter_context(nc.sbuf_tensor("rrow_b", [1, 512], BF16))
            bc_sb = a2.enter_context(nc.sbuf_tensor("bc_sb", [64, 512], BF16))

            pL = [a2.enter_context(nc.psum_tensor("pL%d" % i, [128, 512], F32))
                  for i in range(4)]
            pC = [a2.enter_context(nc.psum_tensor("pC0", [65, 512], F32)),
                  a2.enter_context(nc.psum_tensor("pC1", [65, 512], F32))]
            pB = a2.enter_context(nc.psum_tensor("pB", [64, 512], F32))
            pO = [pL[0], pL[1]]

            vwo = EMIT("g", lambda e: e.dma_start(
                out=wo_sb[:, :].rearrange("p (r d) -> p r d", r=8),
                in_=wo_h.rearrange("(r p) d -> p r d", p=128)), "dWo", 16)

            lctr = 0
            pv_reads = {}
            gate_vals = {}
            wo_waited = False

            def emit_gates(j):
                lg = lg_all[:, j * E:(j + 1) * E]
                eg = rt8[:, 1 * E:2 * E]
                ind = ind_all[:, j * E:(j + 1) * E]
                ind2 = ind2_all[:, j * E:(j + 1) * E]
                u = rt8[:, 4 * E:5 * E]
                p_ = rt8[:, 5 * E:6 * E]
                DR("v")
                EMIT("v", lambda e, lg=lg: e.reduce_max(rs1[:, 0:1], lg, axis=AX.X, negate=True))
                DR("v")
                EMIT("v", lambda e, lg=lg, ind=ind: e.tensor_scalar(
                    ind, lg, rs1[:, 0:1], 0.0, op0=ALU.add, op1=ALU.is_ge))
                DR("v")
                EMIT("v", lambda e, lg=lg, ind=ind, u=u: e.scalar_tensor_tensor(
                    u, ind, -1e30, lg, op0=ALU.mult, op1=ALU.add))
                DR("v")
                vnm2 = EMIT("v", lambda e, u=u: e.reduce_max(
                    rs1[:, 1:2], u, axis=AX.X, negate=True), "tV")
                W("s", "tV", vnm2)
                DR("s")
                vexp = EMIT("s", lambda e, lg=lg, eg=eg: e.activation(
                    eg, lg, AF.Exp, bias=rs1[:, 0:1], scale=1.0), "tS")
                W("v", "tS", vexp)
                EMIT("v", lambda e, lg=lg, ind2=ind2: e.tensor_scalar(
                    ind2, lg, rs1[:, 1:2], 0.0, op0=ALU.add, op1=ALU.is_ge))
                DR("v")
                EMIT("v", lambda e, eg=eg, ind2=ind2, p_=p_: e.tensor_mul(p_, eg, ind2))
                DR("v")
                EMIT("v", lambda e, p_=p_: e.reduce_sum(rs1[:, 2:3], p_, axis=AX.X))
                DR("v")
                EMIT("v", lambda e: e.reciprocal(rs1[:, 3:4], rs1[:, 2:3]))
                DR("v")
                EMIT("v", lambda e, j=j, p_=p_: e.tensor_scalar_mul(
                    gates[:, j * E:(j + 1) * E], p_, rs1[:, 3:4]), "tV")
                DR("v")

            DEPTH = 3
            pending_norm = [None]
            last_vbc = [None]
            for qc in range(NQC):
                ctx_ready = []
                for h in range(NH):
                    hp = h % 2
                    hb = h // 2
                    cpn = "pC%d" % hp
                    nt_ = 4 * qc + 4
                    soff = 8 * hp if nt_ <= 8 else 0
                    for i in range(nt_ + DEPTH):
                        if i < nt_:
                            t = i
                            sl = soff + t
                            qv0 = max(0, 128 * t - 512 * qc)
                            lb = lctr % 4
                            lctr += 1
                            lbn = "pL%d" % lb
                            bank_wait("t", lbn)
                            vlg = EMIT("t", lambda e, t=t, qc=qc, hp=hp, hb=hb, qv0=qv0, lb=lb:
                                       e.matmul(pL[lb][:, qv0:512],
                                                kT[64 * hp:64 * hp + 64,
                                                   hb * S + 128 * t: hb * S + 128 * (t + 1)],
                                                qT[64 * hp:64 * hp + 64,
                                                   hb * S + 512 * qc + qv0: hb * S + 512 * (qc + 1)],
                                                start=True, stop=True), "tT")
                            bank_done("t", lbn, "tT", vlg)
                            bank_wait("s", lbn)
                            if sl in pv_reads:
                                W("s", "tT", pv_reads[sl])
                            vex = EMIT("s", lambda e, sl=sl, qv0=qv0, lb=lb: e.activation(
                                pT[:, 512 * sl + qv0: 512 * (sl + 1)],
                                pL[lb][:, qv0:512], AF.Exp, scale=0.125), "tS")
                            bank_done("s", lbn, "tS", vex)
                            gate_vals[t] = ("tS", vex)
                            if t >= 4 * qc:
                                W("v", "tS", vex)
                                vmask = EMIT("v", lambda e, sl=sl, qv0=qv0: e.tensor_mul(
                                    pT[:, 512 * sl + qv0: 512 * sl + qv0 + 128],
                                    pT[:, 512 * sl + qv0: 512 * sl + qv0 + 128],
                                    tril_b[:, :]), "tV")
                                DR("v")
                                gate_vals[t] = ("tV", vmask)
                        if i == 1 and pending_norm[0] is not None:
                            pending_norm[0]()
                            pending_norm[0] = None
                        if i >= DEPTH:
                            t = i - DEPTH
                            sl = soff + t
                            qv0 = max(0, 128 * t - 512 * qc)
                            gv = gate_vals[t]
                            W("t", gv[0], gv[1])
                            if t == 0:
                                bank_wait("t", cpn)
                            vpv = EMIT("t", lambda e, t=t, sl=sl, h=h, hp=hp, qv0=qv0,
                                       first=(t == 0), last=(t == nt_ - 1):
                                       e.matmul(pC[hp][:, qv0:512],
                                                v_sb[:, t * NH * 65 + 65 * h: t * NH * 65 + 65 * h + 65],
                                                pT[:, 512 * sl + qv0: 512 * (sl + 1)],
                                                start=first, stop=last), "tT")
                            pv_reads[sl] = vpv
                    bank_done("t", cpn, "tT", vpv)
                    bank_wait("s", cpn)
                    if last_vbc[0] is not None:
                        W("s", "tT", last_vbc[0])
                    DR("s")
                    EMIT("s", lambda e, hp=hp: e.activation(rrow_f[:, :], pC[hp][64:65, :], AF.Ln))
                    DR("s")
                    vrb = EMIT("s", lambda e: e.activation(rrow_b[:, :], rrow_f[:, :], AF.Exp,
                                                           scale=-1.0), "tS")
                    bank_done("s", cpn, "tS", vrb)

                    def _norm_tail(hp=hp, hb=hb, cpn=cpn, vrb=vrb):
                        W("t", "tS", vrb)
                        bank_wait("t", "pB")
                        vbc = EMIT("t", lambda e: e.matmul(pB[0:64, :], ones_b[:, 0:64], rrow_b[:, :],
                                                           start=True, stop=True), "tT")
                        last_vbc[0] = vbc
                        bank_done("t", "pB", "tT", vbc)
                        bank_wait("v", "pB")
                        EMIT("v", lambda e: e.tensor_copy(bc_sb[:, :], pB[0:64, :]))
                        DR("v")
                        vnz = EMIT("v", lambda e, hp=hp, hb=hb: e.tensor_mul(
                            ctx_all[64 * hp:64 * hp + 64, 512 * hb: 512 * (hb + 1)],
                            pC[hp][0:64, :], bc_sb[:, :]), "tV")
                        DR("v")
                        bank_done("v", "pB", "tV", vnz)
                        bank_done("v", cpn, "tV", vnz)
                        ctx_ready.append(vnz)
                    pending_norm[0] = _norm_tail
                    if qc == 0:
                        emit_gates(h)
                if pending_norm[0] is not None:
                    pending_norm[0]()
                    pending_norm[0] = None
                if not wo_waited:
                    W("t", "dWo", vwo)
                    wo_waited = True
                for half in range(2):
                    for j4 in range(4):
                        bank_wait("t", "pL0")
                        bank_wait("t", "pL1")
                        W("t", "tV", ctx_ready[-1])
                        def _op(e, j4=j4, half=half):
                            ins = None
                            for h in range(NH):
                                hp = h % 2
                                hb = h // 2
                                ins = e.matmul(
                                    pO[hp][:, :],
                                    ctx_all[64 * hp:64 * hp + 64,
                                            512 * hb + 128 * j4: 512 * hb + 128 * (j4 + 1)],
                                    wo_sb[64 * hp:64 * hp + 64,
                                          1024 * hb + 512 * half: 1024 * hb + 512 * (half + 1)],
                                    start=(h < 2), stop=(h >= NH - 2))
                            return ins
                        vop = EMIT("t", _op, "tT")
                        bank_done("t", "pL0", "tT", vop)
                        bank_done("t", "pL1", "tT", vop)
                        bank_wait("v", "pL0")
                        bank_wait("v", "pL1")
                        jg = 4 * qc + j4
                        EMIT("v", lambda e, jg=jg, half=half: e.tensor_copy(
                            acc[:, jg * D + 512 * half: jg * D + 512 * (half + 1)], pO[0][:, :]))
                        DR("v")
                        voe = EMIT("v", lambda e, jg=jg, half=half: e.tensor_add(
                            acc[:, jg * D + 512 * half: jg * D + 512 * (half + 1)],
                            acc[:, jg * D + 512 * half: jg * D + 512 * (half + 1)],
                            pO[1][:, :]), "tV")
                        DR("v")
                        bank_done("v", "pL0", "tV", voe)
                        bank_done("v", "pL1", "tV", voe)
            emit_prestage(64)
            W("y", "tV", P.sem_val["tV"])
            EMIT("y", lambda e: e.dma_start(
                out=out_h.rearrange("(j p) d -> p j d", p=128),
                in_=acc[:, :].rearrange("p (j d) -> p j d", d=D)), "dOut", 16)
            for ek in ["t", "s", "v"]:
                W(ek, OWN[ek], P.sem_val[OWN[ek]])
                P.add(ek, lambda e: e.nop().then_inc(sem["ph"], 1))
                P.inc("ph")
            W("y", "dA", P.sem_val["dA"])
            P.add("y", lambda e: e.nop().then_inc(sem["ph"], 1))
            P.inc("ph")
            W("g", "dWo", P.sem_val["dWo"])
            W("g", "dPa", P.sem_val.get("dPa", 0))
            P.add("g", lambda e: e.nop().then_inc(sem["ph"], 1))
            bar_a2 = P.inc("ph")
        for ek in ["t", "s", "v", "g", "y"]:
            W(ek, "ph", bar_a2)
        bank_last.clear()
        qkv_scope.close()

        with contextlib.ExitStack() as rt:
            sel_ch = rt.enter_context(nc.sbuf_tensor("sel_ch", [8, 512], F32))
            cum_ch = rt.enter_context(nc.sbuf_tensor("cum_ch", [8, 512], F32))
            pos_ch = rt.enter_context(nc.sbuf_tensor("pos_ch", [8, 512], F32))
            zer_ch = rt.enter_context(nc.sbuf_tensor("zer_ch", [8, 512], F32))
            carry = rt.enter_context(nc.sbuf_tensor("carry", [8, 4], F32))
            pos_tm = rt.enter_context(nc.sbuf_tensor("pos_tm", [128, NT * E], F32))
            nch_c = rt.enter_context(nc.sbuf_tensor("nch_c", [8, 4], F32))
            srow = rt.enter_context(nc.sbuf_tensor("srow", [1, 64], F32))
            mrow = rt.enter_context(nc.sbuf_tensor("mrow", [8, 64], F32))
            mrow_b = rt.enter_context(nc.sbuf_tensor("mrow_b", [8, 64], BF16))
            lsi = rt.enter_context(nc.sbuf_tensor("lsi", [128, 132 * 8], F32))
            sci32 = [rt.enter_context(nc.sbuf_tensor("sci%d" % i, [128, 8], I32)) for i in range(32)]
            sct4 = [rt.enter_context(nc.sbuf_tensor("sct%d" % i, [128, 8], F32)) for i in range(32)]
            rsc = rt.enter_context(nc.sbuf_tensor("rsc", [128, 8], F32))
            pR = rt.enter_context(nc.psum_tensor("pR", [128, 512], F32))
            xr2 = [rt.enter_context(nc.sbuf_tensor("xr%d" % i, [128, D], F32))
                   for i in range(2)]
            scat_done = [None, None]

            xr_free = [None, None]
            for jg in range(NT):
                slot = jg % 2
                dxs = "dX0" if slot == 0 else "dX1"
                if xr_free[slot] is not None:
                    W("y", "dPoA", xr_free[slot])
                vxl = EMIT("y", lambda e, jg=jg, slot=slot: e.dma_start(
                    out=xr2[slot][:, :],
                    in_=x_h.rearrange("(j p) d -> j p d", p=128)[jg]), dxs, 16)
                W("g", dxs, vxl)
                if jg == 0:
                    W("g", "dOut", P.sem_val["dOut"])
                else:
                    W("g", "dPoA", P.sem_val["dPoA"])
                xadd = EMIT("g", lambda e, jg=jg, slot=slot: e.indirect_dma_start(
                    out=out_h,
                    out_offset=bass.IndirectOffsetOnAxis(ap=ixj[:, jg:jg + 1], axis=0),
                    in_=xr2[slot][:, :], in_offset=None,
                    compute_op=ALU.add), "dPoA", 16)
                xr_free[slot] = xadd

            def emit_routing(gates_done_tv):
                lsi3 = lsi[:, :].rearrange("p (c three) -> p c three", three=8)
                EMIT("v", lambda e: e.memset(lsi3[:, :, :], 0.0))
                DR("v")
                vli = EMIT("v", lambda e: e.memset(lsi3[:, :, 1:2], HUGE), "tV")
                W("y", "tV", vli)
                EMIT("y", lambda e: e.dma_start(
                    out=lst_h[0:NCH * 512].rearrange("(c p) eight -> p c eight", p=128),
                    in_=lsi3), "dOut", 16)
                vini = EMIT("y", lambda e: e.dma_start(
                    out=lst_h[NCH * 512:NCH * 512 + 1],
                    in_=lsi[0:1, 0:8]), "dOut", 16)
                EMIT("v", lambda e: e.memset(zer_ch[:, :], 0.0))
                EMIT("v", lambda e: e.memset(carry[:, :], 0.0))
                DR("v")
                for c4 in range(4):
                    W("t", "tV", gates_done_tv)
                    bank_wait("t", "pR")
                    def _trs(e, c4=c4):
                        ins = None
                        for i in range(4):
                            j = 4 * c4 + i
                            ins = e.transpose(pR[0:8, 128 * i:128 * (i + 1)],
                                              ind2_all[:, j * E:(j + 1) * E], id_f[:, :])
                        return ins
                    vt1 = EMIT("t", _trs, "tT")
                    bank_done("t", "pR", "tT", vt1)
                    bank_wait("v", "pR")
                    EMIT("v", lambda e: e.tensor_copy(sel_ch[:, :], pR[0:8, 0:512]))
                    DR("v")
                    init_ap = 0.0 if c4 == 0 else carry[:, c4 - 1:c4]
                    EMIT("v", lambda e, init_ap=init_ap: e.tensor_tensor_scan(
                        cum_ch[:, :], sel_ch[:, :], zer_ch[:, :], init_ap,
                        op0=ALU.add, op1=ALU.add))
                    DR("v")
                    EMIT("v", lambda e: e.tensor_sub(pos_ch[:, :], cum_ch[:, :], sel_ch[:, :]))
                    DR("v")
                    vcar = EMIT("v", lambda e, c4=c4: e.tensor_copy(
                        carry[:, c4:c4 + 1], cum_ch[:, 511:512]), "tV")
                    bank_done("v", "pR", "tV", vcar)
                    DR("v")
                    W("t", "tV", vcar)
                    bank_wait("t", "pR")
                    def _trp(e, c4=c4):
                        ins = None
                        for i in range(4):
                            ins = e.transpose(pR[:, 32 * i:32 * i + 8],
                                              pos_ch[0:8, 128 * i:128 * (i + 1)],
                                              id_f[0:8, 0:8])
                        return ins
                    vt2 = EMIT("t", _trp, "tT")
                    bank_done("t", "pR", "tT", vt2)
                    bank_wait("v", "pR")
                    def _evp(e, c4=c4):
                        ins = None
                        for i in range(4):
                            j = 4 * c4 + i
                            ins = e.tensor_copy(pos_tm[:, j * E:(j + 1) * E],
                                                pR[:, 32 * i:32 * i + 8])
                        return ins
                    vev = EMIT("v", _evp, "tV")
                    bank_done("v", "pR", "tV", vev)
                    DR("v")
                for c in range(4):
                    EMIT("v", lambda e, c=c: e.tensor_scalar(
                        nch_c[:, c:c + 1], carry[:, 3:4], -(512.0 * c + 1.0), 0.0,
                        op0=ALU.add, op1=ALU.is_ge))
                    DR("v")
                vnch = EMIT("v", lambda e: e.reduce_sum(rsc[0:8, 0:1], nch_c[:, 0:4], axis=AX.X), "tV")
                DR("v")
                W("t", "tV", vnch)
                bank_wait("t", "pR")
                vtn = EMIT("t", lambda e: e.transpose(pR[0:1, 0:8], rsc[0:8, 0:1],
                                                      id_f[0:8, 0:8]), "tT")
                bank_done("t", "pR", "tT", vtn)
                bank_wait("v", "pR")
                EMIT("v", lambda e: e.tensor_copy(srow[0:1, 0:8], pR[0:1, 0:8]))
                DR("v")
                EMIT("v", lambda e: e.tensor_tensor_scan(
                    srow[0:1, 8:16], srow[0:1, 0:8], srow[0:1, 0:8], 0.0,
                    op0=ALU.add, op1=ALU.bypass))
                DR("v")
                vcr = EMIT("v", lambda e: e.engine_nop() if False else e.tensor_copy(
                    srow[0:1, 16:24], srow[0:1, 8:16]), "tV")
                bank_done("v", "pR", "tV", vcr)
                W("t", "tV", vcr)
                bank_wait("t", "pR")
                vtc = EMIT("t", lambda e: e.transpose(pR[0:8, 0:1], srow[0:1, 8:16],
                                                      id_f[0:1, 0:1]), "tT")
                bank_done("t", "pR", "tT", vtc)
                bank_wait("v", "pR")
                EMIT("v", lambda e: e.tensor_copy(nch_c[:, 1:2], pR[0:8, 0:1]))
                DR("v")
                EMIT("v", lambda e: e.tensor_scalar(
                    mrow[:, 0:16], iok_b[:, :], nch_c[:, 1:2], 0.0,
                    op0=ALU.subtract, op1=ALU.is_ge))
                DR("v")
                EMIT("v", lambda e: e.tensor_scalar_mul(mrow[:, 16:32], mrow[:, 0:16],
                                                        rsc[0:8, 0:1]))
                DR("v")
                vmb = EMIT("v", lambda e: e.tensor_copy(mrow_b[:, 0:32], mrow[:, 0:32]), "tV")
                bank_done("v", "pR", "tV", vmb)
                W("t", "tV", vmb)
                bank_wait("t", "pR")
                vms = EMIT("t", lambda e: e.matmul(pR[0:1, 0:32], ones8[:, 0:1], mrow_b[:, 0:32],
                                                   start=True, stop=True), "tT")
                bank_done("t", "pR", "tT", vms)
                bank_wait("v", "pR")
                EMIT("v", lambda e: e.tensor_copy(srow[0:1, 32:48], pR[0:1, 0:16]))
                DR("v")
                EMIT("v", lambda e: e.tensor_sub(srow[0:1, 48:64], iok_b[0:1, :],
                                                 pR[0:1, 16:32]))
                DR("v")
                vsm = EMIT("v", lambda e: e.engine_nop() if False else e.tensor_copy(
                    srow[0:1, 16:24], srow[0:1, 8:16]), "tV")
                bank_done("v", "pR", "tV", vsm)
                DR("v")
                EMIT("v", lambda e: e.tensor_scalar_min(srow[0:1, 0:16], srow[0:1, 32:48], 7.0))
                DR("v")
                EMIT("v", lambda e: e.scalar_tensor_tensor(
                    srow[0:1, 16:32], srow[0:1, 32:48], 4.0, srow[0:1, 48:64],
                    op0=ALU.mult, op1=ALU.add))
                DR("v")
                EMIT("v", lambda e: e.tensor_scalar_min(srow[0:1, 16:32], srow[0:1, 16:32], 32.0))
                DR("v")
                EMIT("v", lambda e: e.tensor_copy(exsched[0:1, :], srow[0:1, 0:16]))
                DR("v")
                EMIT("v", lambda e: e.tensor_scalar_mul(mrow[0:1, 0:16], srow[0:1, 0:16], 512.0))
                DR("v")
                EMIT("v", lambda e: e.tensor_scalar_mul(mrow[0:1, 16:32], srow[0:1, 16:32], 512.0))
                DR("v")
                EMIT("v", lambda e: e.tensor_scalar_mul(mrow[0:1, 32:48], srow[0:1, 0:16], 32.0))
                DR("v")
                EMIT("v", lambda e: e.tensor_scalar_mul(mrow[0:1, 48:64], srow[0:1, 0:16], 128.0))
                DR("v")
                vbr = EMIT("v", lambda e: e.tensor_copy(mrow_b[0:1, 0:64], mrow[0:1, 0:64]), "tV")
                W("t", "tV", vbr)
                bank_wait("t", "pR")
                vbm = EMIT("t", lambda e: e.matmul(pR[:, 0:64], ones_b[0:1, 0:128],
                                                   mrow_b[0:1, 0:64], start=True, stop=True), "tT")
                bank_done("t", "pR", "tT", vbm)
                bank_wait("v", "pR")
                EMIT("v", lambda e: e.tensor_copy(bcast_t[:, :], pR[:, 0:64]))
                DR("v")
                for i4 in range(32):
                    EMIT("v", lambda e, i4=i4: e.memset(sct4[i4][:, :], 0.0))
                    EMIT("v", lambda e, i4=i4: e.memset(sci32[i4][:, :], 0))
                DR("v")
                W("g", "dOut", vini)
                for j in range(NT):
                    par = j % 2
                    dyn = "dY%d" % par
                    EMIT("v", lambda e, j=j: e.tensor_sub(
                        rsc[:, 0:8], ind2_all[:, j * E:(j + 1) * E],
                        ind_all[:, j * E:(j + 1) * E]))
                    DR("v")
                    for r in range(2):
                        mask = (lambda jj: ind_all[:, jj * E:(jj + 1) * E])(j) if r == 0 else rsc[:, 0:8]
                        EMIT("v", lambda e, mask=mask: e.tensor_mul(rsc[:, 8:16] if False else rt8[:, 0:8], mask, iota_eb[:, :]))
                        DR("v")
                        EMIT("v", lambda e: e.reduce_sum(rs1[:, 0:1], rt8[:, 0:8], axis=AX.X))
                        DR("v")
                        EMIT("v", lambda e, mask=mask, j=j: e.tensor_mul(
                            rt8[:, 0:8], mask, pos_tm[:, j * E:(j + 1) * E]))
                        DR("v")
                        EMIT("v", lambda e: e.reduce_sum(rs1[:, 1:2], rt8[:, 0:8], axis=AX.X))
                        DR("v")
                        EMIT("v", lambda e, mask=mask, j=j: e.tensor_mul(
                            rt8[:, 0:8], mask, gates[:, j * E:(j + 1) * E]))
                        DR("v")
                        EMIT("v", lambda e, j=j, r=r: e.reduce_sum(sct4[j * 2 + r][:, 2:3],
                                                                   rt8[:, 0:8], axis=AX.X))
                        DR("v")
                        EMIT("v", lambda e, j=j, r=r: e.tensor_scalar_add(
                            sct4[j * 2 + r][:, 0:1], iota_p[:, :], float(128 * j)))
                        DR("v")
                        EMIT("v", lambda e, j=j, r=r: e.tensor_scalar_add(
                            sct4[j * 2 + r][:, 1:2], iota_p[:, :],
                            float(128 * j)))
                        DR("v")
                        EMIT("v", lambda e: e.scalar_tensor_tensor(
                            rs1[:, 2:3], rs1[:, 0:1], 2048.0, rs1[:, 1:2],
                            op0=ALU.mult, op1=ALU.add))
                        DR("v")
                        EMIT("v", lambda e: e.tensor_scalar_min(
                            rs1[:, 2:3], rs1[:, 2:3], float(NCH * 512)))
                        DR("v")
                        vsc = EMIT("v", lambda e, j=j, r=r: e.tensor_copy(sci32[j * 2 + r][:, 0:1],
                                                                          rs1[:, 2:3]), "tV")
                        DR("v")
                        W("g", "tV", vsc)
                        scat_done[par] = EMIT("g", lambda e, j=j, r=r: e.indirect_dma_start(
                            out=lst_h,
                            out_offset=bass.IndirectOffsetOnAxis(ap=sci32[j * 2 + r][:, 0:1], axis=0),
                            in_=sct4[j * 2 + r][:, :], in_offset=None,
                            bounds_check=e._bnd_lst, oob_is_err=False), dyn, 16)
            emit_routing(0)
            for ek in ["t", "v"]:
                W(ek, OWN[ek], P.sem_val[OWN[ek]])
                P.add(ek, lambda e: e.nop().then_inc(sem["ph"], 1))
                P.inc("ph")
            W("g", "dY0", P.sem_val.get("dY0", 0))
            W("g", "dY1", P.sem_val.get("dY1", 0))
            W("g", "dOut", P.sem_val["dOut"])
            W("g", "dPoA", P.sem_val["dPoA"])
            P.add("g", lambda e: e.nop().then_inc(sem["ph"], 1))
            P.inc("ph")
            W("y", "dOut", P.sem_val["dOut"])
            P.add("y", lambda e: e.nop().then_inc(sem["ph"], 1))
            bar_rt = P.inc("ph")
        for ek in ["t", "s", "v", "g", "y"]:
            W(ek, "ph", bar_rt)
        bank_last.clear()
        acc_scope.close()

        with contextlib.ExitStack() as f:
            ls8 = [f.enter_context(nc.sbuf_tensor("ls%d" % i, [128, 8], F32)) for i in range(8)]
            tok_i = f.enter_context(nc.sbuf_tensor("tok_i", [128, 2 * 4], I32))
            dst_i = f.enter_context(nc.sbuf_tensor("dst_i", [128, 2 * 4], I32))
            wix = f.enter_context(nc.sbuf_tensor("wix", [128, 2 * 5], I32))
            wixf = f.enter_context(nc.sbuf_tensor("wixf", [128, 5], F32))
            lsix = f.enter_context(nc.sbuf_tensor("lsix", [128, 2 * 4], I32))
            b1ix = f.enter_context(nc.sbuf_tensor("b1ix", [32, 2], I32))
            ixf = f.enter_context(nc.sbuf_tensor("ixf", [128, 4], F32))
            b2f = f.enter_context(nc.sbuf_tensor("b2f", [2, D], F32))
            b2ix = f.enter_context(nc.sbuf_tensor("b2ix", [2, 1], I32))
            xg4 = [f.enter_context(nc.sbuf_tensor("xg%d" % i, [128, 1024], BF16)) for i in range(4)]
            xgT = f.enter_context(nc.sbuf_tensor("xgT", [128, 2 * ND * 512], BF16))
            hT = f.enter_context(nc.sbuf_tensor("hT", [128, 32 * 512], BF16))
            w1r2 = [f.enter_context(nc.sbuf_tensor("w1r%d" % i, [128, ND * 1024], BF16)) for i in range(2)]
            w2full = f.enter_context(nc.sbuf_tensor("w2full", [128, 32 * D], BF16))
            b1st2 = [f.enter_context(nc.sbuf_tensor("b1st%d" % i, [32, 128], F32)) for i in range(2)]
            b1sb = f.enter_context(nc.sbuf_tensor("b1sb", [128, 2 * 32], F32))
            b2r = f.enter_context(nc.sbuf_tensor("b2r", [1, 2 * D], BF16))

            ysb2 = [f.enter_context(nc.sbuf_tensor("ysb%d" % i, [128, D], F32)) for i in range(4)]
            pH = [f.enter_context(nc.psum_tensor("pH%d" % i, [128, 512], F32)) for i in range(4)]
            pbT2 = f.enter_context(nc.psum_tensor("pbT2", [128, 1024], BF16))

            ls_free = [None, None]
            b2f_free = [None]
            xg_free = [None] * 4
            xgT_free = [None, None]
            w1_free = [None, None]
            w2_free = [None, None]
            ysb_free = [None, None, None, None]
            ls_n = [0, 0]
            g_n = [0, 0]
            w1_n = [0, 0]
            w2_n = [0, 0]
            w1ctr_l = [0]
            w2ctr = 0
            y_n = {"dY0": P.sem_val.get("dY0", 0) // 16, "dY1": P.sem_val.get("dY1", 0) // 16}
            rt_tv = P.sem_val["tV"]

            b1_ready_l = [None, None]
            b1st_free = [None, None]
            b1sb_free = [None, None]
            w1q_infos = [None, None]
            load_w1q_l = [None, None]

            def front(k):
                kp = k % 2
                dls = "dLs%d" % kp
                dg = "dG%d" % kp
                if ls_free[kp] is not None:
                    W("v", ls_free[kp][0], ls_free[kp][1])
                if k < 2:
                    pass
                DR("v")
                for c in range(4):
                    EMIT("v", lambda e, k=k, c=c: e.tensor_scalar(
                        ixf[:, c:c + 1], iota_p[:, :], bcast_t[:, 16 + k:17 + k],
                        float(128 * c), op0=ALU.add, op1=ALU.add))
                DR("v")
                EMIT("v", lambda e, kp=kp: e.tensor_copy(lsix[:, kp * 4:(kp + 1) * 4], ixf[:, 0:4]))
                DR("v")
                for c in range(4):
                    EMIT("v", lambda e, k=k, c=c: e.tensor_scalar(
                        wixf[:, c:c + 1], iota_p[:, :], bcast_t[:, k:k + 1],
                        float(128 * c), op0=ALU.add, op1=ALU.add))
                EMIT("v", lambda e, k=k: e.tensor_scalar(
                    wixf[:, 4:5], iota_p[:, :], bcast_t[:, 48 + k:49 + k],
                    None, op0=ALU.add))
                DR("v")
                EMIT("v", lambda e, kp=kp: e.tensor_copy(wix[:, kp * 5:kp * 5 + 5], wixf[:, 0:5]))
                DR("v")
                EMIT("v", lambda e, k=k: e.tensor_scalar(
                    ixf[0:32, 0:1], iota_p[0:32, :], bcast_t[0:32, 32 + k:33 + k],
                    None, op0=ALU.add))
                DR("v")
                EMIT("v", lambda e, kp=kp: e.tensor_copy(b1ix[:, kp:kp + 1],
                                                         ixf[0:32, 0:1]))
                DR("v")
                EMIT("v", lambda e, k=k: e.tensor_scalar_mul(
                    ixf[0:2, 1:2], bcast_t[0:2, k:k + 1], 1.0 / 512.0))
                DR("v")
                vidx = EMIT("v", lambda e: e.tensor_copy(b2ix[:, 0:1], ixf[0:2, 1:2]), "tV")
                DR("v")
                W("g", "tV", vidx)
                ls_n[kp] += 6
                for c in range(4):
                    EMIT("g", lambda e, kp=kp, c=c: e.indirect_dma_start(
                        out=ls8[kp * 4 + c][:, :],
                        out_offset=None, in_=lst_h,
                        in_offset=bass.IndirectOffsetOnAxis(ap=lsix[:, kp * 4 + c:kp * 4 + c + 1], axis=0)),
                        dls, 16)
                if b1st_free[kp] is not None:
                    W("g", b1st_free[kp][0], b1st_free[kp][1])
                EMIT("g", lambda e, kp=kp: e.indirect_dma_start(
                    out=b1st2[kp][:, :], out_offset=None,
                    in_=b1_h.rearrange("e (a b) -> (e a) b", b=128),
                    in_offset=bass.IndirectOffsetOnAxis(ap=b1ix[:, kp:kp + 1], axis=0)),
                    dls, 16)
                if b2f_free[0] is not None:
                    W("g", b2f_free[0][0], b2f_free[0][1])
                EMIT("g", lambda e: e.indirect_dma_start(
                    out=b2f[:, :], out_offset=None,
                    in_=b2_h,
                    in_offset=bass.IndirectOffsetOnAxis(ap=b2ix[:, 0:1], axis=0)),
                    dls, 16)
                W("v", dls, 16 * ls_n[kp])
                if g_n[kp] > 0:
                    W("v", dg, 16 * g_n[kp])
                W("v", "dY0", 16 * y_n["dY0"])
                W("v", "dY1", 16 * y_n["dY1"])
                DR("v")
                for c in range(4):
                    EMIT("v", lambda e, kp=kp, c=c: e.tensor_copy(
                        tok_i[:, kp * 4 + c:kp * 4 + c + 1], ls8[kp * 4 + c][:, 0:1]))
                DR("v")
                vb2c = EMIT("v", lambda e, kp=kp: e.tensor_copy(
                    b2r[:, kp * D:(kp + 1) * D], b2f[0:1, 0:D]), "tV")
                b2f_free[0] = ("tV", vb2c)
                DR("v")
                for c in range(3):
                    EMIT("v", lambda e, kp=kp, c=c: e.tensor_copy(
                        dst_i[:, kp * 4 + c:kp * 4 + c + 1], ls8[kp * 4 + c][:, 1:2]))
                DR("v")
                vdst = EMIT("v", lambda e, kp=kp: e.tensor_copy(
                    dst_i[:, kp * 4 + 3:kp * 4 + 4], ls8[kp * 4 + 3][:, 1:2]), "tV")
                DR("v")
                ls_free[kp] = ("tV", vdst)
                W("g", "tV", vdst)
                for c in range(4):
                    if xg_free[c] is not None:
                        W("g", xg_free[c][0], xg_free[c][1])
                    g_n[kp] += 1
                    EMIT("g", lambda e, kp=kp, c=c: e.indirect_dma_start(
                        out=xg4[c][:, :], out_offset=None,
                        in_=xnd_h,
                        in_offset=bass.IndirectOffsetOnAxis(ap=tok_i[:, kp * 4 + c:kp * 4 + c + 1], axis=0)),
                        dg, 16)
                w1q_info = [None] * 4
                w1q_infos[kp] = w1q_info

                def load_w1q(q, kp=kp):
                    wp = w1ctr_l[0] % 2
                    w1ctr_l[0] += 1
                    if w1_free[wp] is not None:
                        W("g", w1_free[wp][0], w1_free[wp][1])
                    w1_n[wp] += 1
                    EMIT("g", lambda e, kp=kp, q=q, wp=wp: e.indirect_dma_start(
                        out=w1r2[wp][:, :], out_offset=None,
                        in_=w1bf_h,
                        in_offset=bass.IndirectOffsetOnAxis(
                            ap=wix[:, kp * 5 + q:kp * 5 + q + 1], axis=0)),
                        "dW1%s" % ("a" if wp == 0 else "b"), 16)
                    w1q_info[q] = (wp, 16 * w1_n[wp])

                load_w1q(0)
                load_w1q(1)
                load_w1q_l[kp] = load_w1q


            def load_w2(k):
                kp = k % 2
                if w2_free[0] is not None:
                    W("g", w2_free[0][0], w2_free[0][1])
                w2_n[0] += 1
                EMIT("g", lambda e, kp=kp: e.indirect_dma_start(
                    out=w2full[:, :], out_offset=None,
                    in_=w2bf_h,
                    in_offset=bass.IndirectOffsetOnAxis(
                        ap=wix[:, kp * 5 + 4:kp * 5 + 5], axis=0)),
                    "dW2a", 16)

            for k in range(TMAX):
                kp = k % 2
                dls = "dLs%d" % kp
                dg = "dG%d" % kp
                if k == 0:
                    front(0)
                    load_w2(0)
                w1q_info = w1q_infos[kp]
                bank_wait("t", "pH0")
                W("t", dls, 16 * ls_n[kp])
                vtr = EMIT("t", lambda e, kp=kp: e.transpose(
                    pH[0][:, 0:32], b1st2[kp][:, :], id_f[0:32, 0:32]), "tT")
                b1st_free[kp] = ("tT", vtr)
                bank_done("t", "pH0", "tT", vtr)
                bank_wait("v", "pH0")
                if b1sb_free[kp] is not None:
                    W("v", b1sb_free[kp][0], b1sb_free[kp][1])
                DR("v")
                vb1e = EMIT("v", lambda e, kp=kp: e.tensor_copy(
                    b1sb[:, kp * 32:(kp + 1) * 32], pH[0][:, 0:32]), "tV")
                DR("v")
                bank_done("v", "pH0", "tV", vb1e)
                b1_ready = vb1e
                last_xgT = None
                for c in range(4):
                    bank_wait("t", "pbT2")
                    W("t", dg, 16 * g_n[kp])
                    if c == 0 and xgT_free[kp] is not None:
                        W("t", xgT_free[kp][0], xgT_free[kp][1])
                    def _trc(e, kp=kp, c=c):
                        ins = None
                        for d in range(ND):
                            ins = e.transpose(pbT2[:, d * 128:(d + 1) * 128],
                                              xg4[c][:, d * 128:(d + 1) * 128],
                                              id_b[:, :])
                        return ins
                    vtc = EMIT("t", _trc, "tT")
                    bank_done("t", "pbT2", "tT", vtc)
                    xg_free[c] = ("tT", vtc)
                    bank_wait("v", "pbT2")
                    DR("v")
                    def _evc(e, kp=kp, c=c):
                        ins = None
                        for d in range(ND):
                            ins = e.tensor_copy(
                                xgT[:, kp * ND * 512 + d * 512 + c * 128:
                                    kp * ND * 512 + d * 512 + (c + 1) * 128],
                                pbT2[:, d * 128:(d + 1) * 128])
                        return ins
                    vev = EMIT("v", _evc, "tV")
                    DR("v")
                    bank_done("v", "pbT2", "tV", vev)
                    last_xgT = vev

                hT_evacs = []
                for q in range(4):
                    wp, wv_ = w1q_info[q]
                    W("t", "dW1%s" % ("a" if wp == 0 else "b"), wv_)
                    vw1m = None
                    for ft in range(8):
                        bank = 1 + (q * 8 + ft) % 3
                        bank_wait("t", "pH%d" % bank)
                        if q == 0 and ft == 0:
                            W("t", "tV", last_xgT)
                        def _w1(e, kp=kp, wp=wp, ft=ft, bank=bank):
                            ins = None
                            for d in range(ND):
                                ins = e.matmul(
                                    pH[bank][:, :],
                                    w1r2[wp][:, d * 1024 + ft * 128: d * 1024 + (ft + 1) * 128],
                                    xgT[:, kp * ND * 512 + d * 512: kp * ND * 512 + (d + 1) * 512],
                                    start=(d == 0), stop=(d == ND - 1))
                            return ins
                        vw1m = EMIT("t", _w1, "tT")
                        bank_done("t", "pH%d" % bank, "tT", vw1m)
                        ekey = "s" if ft % 2 == 0 else "v"
                        bank_wait(ekey, "pH%d" % bank)
                        if ekey == "s":
                            W("s", "tV", b1_ready)
                            vev = EMIT("s", lambda e, q=q, ft=ft, kp=kp, bank=bank: e.activation(
                                hT[:, (q * 8 + ft) * 512:(q * 8 + ft + 1) * 512],
                                pH[bank][:, :], AF.Relu,
                                bias=b1sb[:, kp * 32 + 8 * q + ft: kp * 32 + 8 * q + ft + 1]), "tS")
                        else:
                            vev = EMIT("v", lambda e, q=q, ft=ft, kp=kp, bank=bank: e.tensor_scalar(
                                hT[:, (q * 8 + ft) * 512:(q * 8 + ft + 1) * 512],
                                pH[bank][:, :],
                                b1sb[:, kp * 32 + 8 * q + ft: kp * 32 + 8 * q + ft + 1], 0.0,
                                op0=ALU.add, op1=ALU.max), "tV")
                        bank_done(ekey, "pH%d" % bank, OWN[ekey], vev)
                        hT_evacs.append((ekey, vev))
                    w1_free[wp] = ("tT", vw1m)
                    if q == 3:
                        sev = [v for (ek2, v) in hT_evacs if ek2 == "s"]
                        if sev:
                            b1sb_free[kp] = ("tS", sev[-1])
                    if q + 2 < 4:
                        load_w1q_l[kp](q + 2)

                if k + 1 < TMAX:
                    front(k + 1)
                ym = {"dY0": y_n["dY0"], "dY1": y_n["dY1"]}
                for sh in range(2):
                    for b in range(4):
                        bank_wait("t", "pH%d" % b)
                    if sh == 0:
                        for (ekey, vev) in hT_evacs:
                            W("t", OWN[ekey], vev)
                    if sh == 0:
                        W("t", "dW2a", 16 * w2_n[0])
                    def _y(e, kp=kp, sh=sh):
                        ins = None
                        for gf in range(32):
                            for ci in range(2):
                                c = sh * 2 + ci
                                for dh in range(2):
                                    ins = e.matmul(
                                        pH[ci * 2 + dh][:, :],
                                        hT[:, gf * 512 + 128 * c: gf * 512 + 128 * (c + 1)],
                                        w2full[:, gf * D + 512 * dh: gf * D + 512 * (dh + 1)],
                                        start=(gf == 0), stop=False)
                        return ins
                    vym = EMIT("t", _y, "tT")
                    if sh == 1:
                        w2_free[0] = ("tT", vym)
                    for b in range(4):
                        bank_done("t", "pH%d" % b, "tT", vym)
                    def _b2(e, kp=kp):
                        ins = None
                        for ci in range(2):
                            for dh in range(2):
                                ins = e.matmul(
                                    pH[ci * 2 + dh][:, :],
                                    ones_b[:, 0:128],
                                    b2r[:, kp * D + 512 * dh: kp * D + 512 * (dh + 1)],
                                    start=False, stop=True)
                        return ins
                    vym = EMIT("t", _b2, "tT")
                    for b in range(4):
                        bank_done("t", "pH%d" % b, "tT", vym)
                    for ci in range(2):
                        c = sh * 2 + ci
                        yp = c
                        dyn = "dY%d" % (c % 2)
                        for b in (ci * 2, ci * 2 + 1):
                            bank_wait("v", "pH%d" % b)
                        if sh == 0 and ci == 0 and ysb_free[0] is not None:
                            W("v", "dY0", 16 * ym["dY0"])
                            W("v", "dY1", 16 * ym["dY1"])
                        DR("v")
                        def _ge(e, kp=kp, c=c, ci=ci, yp=yp):
                            ins = None
                            for dh in range(2):
                                ins = e.tensor_scalar(
                                    ysb2[yp][:, 512 * dh: 512 * (dh + 1)],
                                    pH[ci * 2 + dh][:, :],
                                    ls8[kp * 4 + c][:, 2:3],
                                    None, op0=ALU.mult)
                            return ins
                        vge = EMIT("v", _ge, "tV")
                        DR("v")
                        for b in (ci * 2, ci * 2 + 1):
                            bank_done("v", "pH%d" % b, "tV", vge)
                        ysb_free[yp] = ("tV", vge)
                        ls_free[kp] = ("tV", vge)
                        W("g", "tV", vge)
                        W("g", "dY0", 16 * ym["dY0"])
                        W("g", "dY1", 16 * ym["dY1"])
                        y_n[dyn] += 1
                        EMIT("g", lambda e, kp=kp, c=c, yp=yp: e.indirect_dma_start(
                            out=out_h,
                            out_offset=bass.IndirectOffsetOnAxis(
                                ap=dst_i[:, kp * 4 + c: kp * 4 + c + 1], axis=0),
                            in_=ysb2[yp][:, :], in_offset=None,
                            bounds_check=e._bnd_yslt, oob_is_err=False,
                            compute_op=ALU.add), dyn, 16)
                if k + 1 < TMAX:
                    load_w2(k + 1)

            for ek in ["t", "s", "v"]:
                W(ek, OWN[ek], P.sem_val[OWN[ek]])
                P.add(ek, lambda e: e.nop().then_inc(sem["ph"], 1))
                P.inc("ph")
            W("y", "dA", P.sem_val["dA"])
            P.add("y", lambda e: e.nop().then_inc(sem["ph"], 1))
            P.inc("ph")
            for sn in ["dW1a", "dW1b", "dW2a", "dW2b", "dLs0", "dLs1",
                       "dG0", "dG1", "dY0", "dY1", "dOut", "dPoA"]:
                W("g", sn, P.sem_val.get(sn, 0))
            P.add("g", lambda e: e.nop().then_inc(sem["ph"], 1))
            bar_f = P.inc("ph")
        for ek in ["t", "s", "v", "g", "y"]:
            W(ek, "ph", bar_f)

        @block.tensor
        def _(eng):
            for fn in P.steps["t"]:
                fn(eng)

        @block.scalar
        def _(eng):
            for fn in P.steps["s"]:
                fn(eng)

        @block.vector
        def _(eng):
            for fn in P.steps["v"]:
                fn(eng)

        @block.gpsimd
        def _(eng):
            for fn in P.steps["g"]:
                fn(eng)

        @block.sync
        def _(eng):
            for fn in P.steps["y"]:
                fn(eng)

    return nc


def kernel(L0_attn_x, L0_attn_wq, L0_attn_wk, L0_attn_wv, L0_attn_wo,
           L0_ffn_Wr, L0_ffn_br, L0_ffn_W1, L0_ffn_b1, L0_ffn_W2, L0_ffn_b2):
    x = np.ascontiguousarray(np.asarray(L0_attn_x, dtype=np.float32))
    B = x.shape[0]
    shared = dict(
        wq=np.ascontiguousarray(np.asarray(L0_attn_wq, np.float32)),
        wk=np.ascontiguousarray(np.asarray(L0_attn_wk, np.float32)),
        wv=np.ascontiguousarray(np.asarray(L0_attn_wv, np.float32)),
        wo=np.ascontiguousarray(np.asarray(L0_attn_wo, np.float32)),
        wr=np.ascontiguousarray(np.asarray(L0_ffn_Wr, np.float32)),
        br=np.ascontiguousarray(np.asarray(L0_ffn_br, np.float32)).reshape(1, E),
        w1=np.ascontiguousarray(np.asarray(L0_ffn_W1, np.float32)),
        b1=np.ascontiguousarray(np.asarray(L0_ffn_b1, np.float32)),
        w2=np.ascontiguousarray(np.asarray(L0_ffn_W2, np.float32)),
        b2=np.ascontiguousarray(np.asarray(L0_ffn_b2, np.float32)),
        ident=np.eye(128, dtype=np.float32),
        tril=np.triu(np.ones((128, 128), np.float32)),
    )
    nc = build_nc()
    in_maps = [dict(x=x[c], **shared) for c in range(B)]
    res = run_bass_kernel_spmd(nc, in_maps, core_ids=list(range(B)))
    out = np.stack([res.results[c]["out"] for c in range(B)], axis=0)
    return out
